# version 02
# baseline (speedup 1.0000x reference)
import contextlib
import numpy as np
import ml_dtypes
import concourse.bass as bass
import concourse.mybir as mybir
from concourse.alu_op_type import AluOpType as OP
from concourse.bass_utils import run_bass_kernel_spmd
from concourse.bass import IndirectOffsetOnAxis

F32 = mybir.dt.float32
BF16 = mybir.dt.bfloat16
U32 = mybir.dt.uint32
AF = mybir.ActivationFunctionType
AX = mybir.AxisListType
BF = ml_dtypes.bfloat16

D = 1024
T = 2048
NDT = 8
NCH = 4
CH = 512
NTT = 16
NCORES = 8

ENGS = ("t", "v", "s", "g", "q")


class Prog:
    def __init__(self, name):
        self.name = name
        self.nc = bass.Bass("TRN2", target_bir_lowering=False, debug=False)
        self.ctx = contextlib.ExitStack()
        self.lam = {e: [] for e in ENGS}
        self.cnt = {}
        self.semkeys = []
        self.sem = {}
        self.consts = {}
        self._uid = 0

    def uid(self):
        self._uid += 1
        return self._uid

    def dram(self, name, shape, dt=F32, kind="Internal"):
        return self.nc.dram_tensor(name, list(shape), dt, kind=kind).ap()

    def inp(self, name, shape, dt=F32):
        return self.dram(name, shape, dt, kind="ExternalInput")

    def out(self, name, shape, dt=F32):
        return self.dram(name, shape, dt, kind="ExternalOutput")

    def const(self, name, arr):
        arr = np.ascontiguousarray(arr)
        dt = {np.dtype(np.float32): F32, np.dtype(BF): BF16,
              np.dtype(np.uint32): U32}[arr.dtype]
        self.consts[name] = arr
        return self.inp(name, arr.shape, dt)

    def sbuf(self, name, shape, dt=F32):
        return self.ctx.enter_context(self.nc.sbuf_tensor(name, list(shape), dt))

    def sbuf_tmp(self, name, shape, dt=F32):
        cm = self.nc.sbuf_tensor(name, list(shape), dt)
        h = cm.__enter__()
        return h, cm

    def psum(self, name, shape, dt=F32):
        return self.ctx.enter_context(self.nc.psum_tensor(name, list(shape), dt))

    def key(self, k):
        if k not in self.cnt:
            self.cnt[k] = 0
            self.semkeys.append(k)
        return k

    def bump(self, k, n=1):
        self.key(k)
        self.cnt[k] += n
        return self.cnt[k]

    def val(self, k):
        self.key(k)
        return self.cnt[k]

    def on(self, eng, fn):
        self.lam[eng].append(fn)

    def wait(self, eng, k, n):
        self.key(k)
        assert n <= self.cnt[k], f"wait {k}>={n} but max {self.cnt[k]}"
        if n > 0:
            self.on(eng, lambda e, _k=k, _n=n: e.wait_ge(self.sem[_k], _n))

    def wait_now(self, eng, k):
        self.wait(eng, k, self.val(k))

    def inc(self, eng, k, fn, n=1):
        c = self.bump(k, n)
        def _f(e, _fn=fn, _k=k, _n=n):
            _fn(e).then_inc(self.sem[_k], _n)
        self.on(eng, _f)
        return c

    def run(self, eng, fn):
        self.on(eng, fn)

    def dma(self, eng, out_ap, in_ap, k=None, **kw):
        k = k or ("d" + eng)
        return self.inc(eng, k, lambda e: e.dma_start(out=out_ap, in_=in_ap, **kw), 16)

    def finalize(self):
        nc = self.nc
        for k in self.semkeys:
            self.sem[k] = self.ctx.enter_context(nc.semaphore("sem_" + k))
        block = self.ctx.enter_context(nc.Block())

        def runner(lams):
            def _r(e):
                for f in lams:
                    f(e)
            return _r

        block.sync(runner(self.lam["q"]))
        block.tensor(runner(self.lam["t"]))
        block.vector(runner(self.lam["v"]))
        block.scalar(runner(self.lam["s"]))
        block.gpsimd(runner(self.lam["g"]))
        return self.nc


def emit_cumsum_free(p, buf, tmp, width):
    cur, oth = buf, tmp
    sh = 1
    while sh < width:
        p.run("v", lambda e, cur=cur, oth=oth, sh=sh: e.tensor_copy(oth[:, 0:sh], cur[:, 0:sh]))
        p.run("v", lambda e, cur=cur, oth=oth, sh=sh: e.tensor_tensor(oth[:, sh:width], cur[:, sh:width], cur[:, 0:width - sh], OP.add))
        p.run("v", lambda e: e.drain())
        cur, oth = oth, cur
        sh *= 2
    if cur is not buf:
        p.run("v", lambda e, cur=cur: e.tensor_copy(buf[:, 0:width], cur[:, 0:width]))
        p.run("v", lambda e: e.drain())



def xslc(i, ch=None):
    if ch is None:
        return slice(i * T, (i + 1) * T)
    return slice(i * T + ch * CH, i * T + (ch + 1) * CH)


def setup_common(p: Prog, need_x=True, need_xck=False, scb_cols=6 * CH, scf_cols=1024):
    c = {}
    if need_x:
        c["X"] = p.sbuf("X", [128, NDT * T], F32)
    if need_xck:
        c["XCK"] = p.sbuf("XCK", [128, NDT * CH], F32)
    c["XLNC"] = p.sbuf("XLNC", [128, NDT * CH], BF16)
    c["SCB"] = p.sbuf("SCB", [128, scb_cols], BF16)
    if scf_cols:
        c["SCF"] = p.sbuf("SCF", [128, scf_cols], F32)
    c["ROWS"] = p.sbuf("ROWS", [128, 3 * CH], F32)
    c["COL"] = p.sbuf("COL", [128, 16], F32)
    c["IDB"] = p.sbuf("IDB", [128, 128], BF16)
    c["IDF"] = p.sbuf("IDF", [128, 128], F32)
    c["ON1B"] = p.sbuf("ON1B", [1, 128], BF16)
    c["ON1F"] = p.sbuf("ON1F", [1, 128], F32)
    c["ONKB"] = p.sbuf("ONKB", [128, 1], BF16)
    for i in range(6):
        c["PS%d" % i] = p.psum("PS%d" % i, [128, 512], F32)
    c["PSTB"] = p.psum("PSTB", [128, 1024], BF16)
    p.c = c

    identb = p.const("identb", np.eye(128, dtype=np.float32).astype(BF))
    identf = p.const("identf", np.eye(128, dtype=np.float32))
    ones1b = p.const("ones1b", np.ones((1, 128), np.float32).astype(BF))
    ones1f = p.const("ones1f", np.ones((1, 128), np.float32))
    oneskb = p.const("oneskb", np.ones((128, 1), np.float32).astype(BF))
    p.dma("q", c["IDB"][:], identb)
    p.dma("q", c["IDF"][:], identf)
    p.dma("q", c["ON1B"][:], ones1b)
    p.dma("q", c["ON1F"][:], ones1f)
    p.dma("q", c["ONKB"][:], oneskb)
    vals = [1e-5, 1.0, 0.0, 1e-5, 1e-6, 1e-12, 1e-8, 0.0, -0.5, -1.0]
    for j, vv in enumerate(vals):
        p.run("v", lambda e, j=j, vv=vv: e.memset(c["COL"][:, j:j + 1], vv))
    p.inc("v", "v", lambda e: e.drain())
    return c


def emit_load_x(p, xfm):
    c = p.c
    for i in range(NDT):
        p.dma("q", c["X"][:, xslc(i)], xfm[i * 128:(i + 1) * 128, :])


def emit_store_x(p, xfm):
    c = p.c
    p.wait_now("q", "v")
    p.wait_now("q", "s")
    for i in range(NDT):
        p.dma("q", xfm[i * 128:(i + 1) * 128, :], c["X"][:, xslc(i)])



def emit_ln_to_hbm(p, xlnh, xdram=None):
    c = p.c
    u = p.uid()
    if xdram is not None:
        XCK = c["XCK"]
        def getx(i, ch):
            return XCK[:, i * CH:(i + 1) * CH]
    else:
        def getx(i, ch):
            return c["X"][:, xslc(i, ch)]
    XB = p.sbuf(f"lnxb{u}", [128, 2 * CH], BF16)
    SQ = p.sbuf(f"lnsq{u}", [128, 2 * CH], BF16)
    MUB = c["SCB"][:, 0:CH]
    RSB = c["SCB"][:, CH:2 * CH]
    ps_s, ps_q, ps_b = c["PS0"], c["PS1"], c["PS2"]
    mu = c["ROWS"][0:1, 0:CH]
    m2 = c["ROWS"][0:1, CH:2 * CH]
    tmp = c["ROWS"][0:1, 2 * CH:3 * CH]
    p.wait_now("t", "dq")
    p.wait_now("v", "dq")
    p.wait_now("s", "dq")
    tprev = {}
    rowread = 0
    bcast2 = 0
    for ch in range(NCH):
        if xdram is not None:
            p.wait("q", "v", p.val("v"))
            p.wait("q", "s", p.val("s"))
            p.wait("q", "dq", p.val("dq"))
            xk = p.dma("q", c["XCK"][:].rearrange("p (i t) -> p i t", i=NDT),
                       xdram.rearrange("(i p) tt -> p i tt", p=128)[:, :, ch * CH:(ch + 1) * CH])
            p.wait("v", "dq", xk)
            p.wait("s", "dq", xk)
        if ch > 0:
            p.wait("t", "v", rowread)
        for i in range(NDT):
            b = i % 2
            bs = slice(b * CH, (b + 1) * CH)
            if (ch, i) != (0, 0) and (i >= 2 or ch > 0):
                k = tprev.get((ch, i - 2)) if i >= 2 else tprev.get((ch - 1, NDT - 2 + i))
                if k:
                    p.wait("v", "t", k)
                    p.wait("s", "t", k)
            cs = p.inc("v", "v", lambda e, i=i, ch=ch, bs=bs: e.tensor_copy(XB[:, bs], getx(i, ch)))
            sq = p.inc("s", "s", lambda e, i=i, ch=ch, bs=bs: e.activation(SQ[:, bs], getx(i, ch), AF.Square))
            p.wait("t", "v", cs)
            p.wait("t", "s", sq)
            p.run("t", lambda e, i=i, bs=bs: e.matmul(ps_s[0:1, :], c["ONKB"][:], XB[:, bs], start=(i == 0), stop=(i == NDT - 1)))
            tm = p.inc("t", "t", lambda e, i=i, bs=bs: e.matmul(ps_q[0:1, :], c["ONKB"][:], SQ[:, bs], start=(i == 0), stop=(i == NDT - 1)))
            tprev[(ch, i)] = tm
        p.wait("v", "t", tm)
        p.run("v", lambda e: e.tensor_scalar(mu, ps_s[0:1, :], 1.0 / D, None, OP.mult))
        p.run("v", lambda e: e.tensor_scalar(m2, ps_q[0:1, :], 1.0 / D, None, OP.mult))
        p.run("v", lambda e: e.tensor_tensor(tmp, mu, mu, OP.mult))
        vv = p.inc("v", "v", lambda e: e.tensor_tensor(m2, m2, tmp, OP.subtract))
        rowread = vv
        p.wait("s", "v", vv)
        p.run("s", lambda e: e.activation(m2, m2, AF.Ln, bias=c["COL"][0:1, 0:1]))
        sr = p.inc("s", "s", lambda e: e.activation(m2, m2, AF.Exp, scale=c["COL"][0:1, 8:9]))
        p.wait("t", "s", sr)
        if bcast2:
            p.wait("t", "v", bcast2)
        tb = p.inc("t", "t", lambda e: e.matmul(ps_b[:, :], c["ON1F"][:], mu, start=True, stop=True))
        p.wait("v", "t", tb)
        ev = p.inc("v", "v", lambda e: e.tensor_copy(MUB, ps_b[:, :]))
        p.wait("t", "v", ev)
        tb2 = p.inc("t", "t", lambda e: e.matmul(ps_b[:, :], c["ON1F"][:], m2, start=True, stop=True))
        p.wait("v", "t", tb2)
        bcast2 = p.inc("v", "v", lambda e: e.tensor_copy(RSB, ps_b[:, :]))
        p.wait("v", "dq", p.val("dq"))
        for i in range(NDT):
            p.run("v", lambda e, i=i, ch=ch: e.tensor_tensor(c["XLNC"][:, i * CH:(i + 1) * CH], getx(i, ch), MUB, OP.subtract))
            vz = p.inc("v", "v", lambda e, i=i: e.tensor_tensor(c["XLNC"][:, i * CH:(i + 1) * CH], c["XLNC"][:, i * CH:(i + 1) * CH], RSB, OP.mult))
            p.wait("q", "v", vz)
            p.dma("q", xlnh[i * 128:(i + 1) * 128, ch * CH:(ch + 1) * CH], c["XLNC"][:, i * CH:(i + 1) * CH])



def emit_gate_top2(p, tag, WgT_sb, wd_dram, mode, xlnh, dbg=None):
    c = p.c
    u = p.uid()
    PSG, PST = c["PS3"], c["PS4"]
    LG = c["SCF"][0:8, 0:CH]
    WD = c["SCF"][0:8, CH:2 * CH]
    LT = p.sbuf(f"lt{u}", [128, 8], F32)
    L2 = p.sbuf(f"l2{u}", [128, 8], F32)
    M1 = p.sbuf(f"m1{u}", [128, 8], F32)
    M2 = p.sbuf(f"m2{u}", [128, 8], F32)
    EE = p.sbuf(f"ee{u}", [128, 8], F32)
    CC = p.sbuf(f"cc{u}", [128, 16], F32)
    XC = c["XLNC"]

    for ch in range(NCH):
        p.wait("q", "t", p.val("t"))
        p.wait("q", "dq", p.val("dq"))
        dk = None
        for i in range(NDT):
            dk = p.dma("q", XC[:, i * CH:(i + 1) * CH], xlnh[i * 128:(i + 1) * 128, ch * CH:(ch + 1) * CH])
        p.wait("t", "dq", dk)
        p.wait("t", "v", p.val("v"))
        for i in range(NDT):
            tg = p.inc("t", "t", lambda e, i=i: e.matmul(PSG[0:8, :], WgT_sb[:, i * 8:(i + 1) * 8], XC[:, i * CH:(i + 1) * CH], start=(i == 0), stop=(i == NDT - 1)))
        p.wait("v", "t", tg)
        ev = p.inc("v", "v", lambda e: e.tensor_copy(LG, PSG[0:8, :]))
        if dbg is not None:
            p.wait("q", "v", ev)
            p.dma("q", dbg[:, ch * CH:(ch + 1) * CH], LG)
        for q in range(4):
            tsl = slice(q * 128, (q + 1) * 128)
            p.wait("t", "v", ev if q == 0 else p.val("v"))
            tr = p.inc("t", "t", lambda e, tsl=tsl: e.transpose(PST[:, 0:8], LG[:, tsl], c["IDF"][0:8, 0:8]))
            p.wait("v", "t", tr)
            def vd(fn):
                p.run("v", fn)
                p.run("v", lambda e: e.drain())
            vd(lambda e: e.tensor_copy(LT[:], PST[0:128, 0:8]))
            vd(lambda e: e.tensor_reduce(CC[:, 0:1], LT[:], AX.X, OP.max))
            vd(lambda e: e.tensor_scalar(M1[:], LT[:], CC[:, 0:1], None, OP.is_ge))
            vd(lambda e: e.scalar_tensor_tensor(L2[:], M1[:], -1e30, LT[:], OP.mult, OP.add))
            vd(lambda e: e.tensor_reduce(CC[:, 1:2], L2[:], AX.X, OP.max))
            vd(lambda e: e.tensor_scalar(M2[:], L2[:], CC[:, 1:2], None, OP.is_ge))
            vm = p.inc("v", "v", lambda e: e.drain())
            if mode == "l0":
                vd(lambda e: e.tensor_scalar(CC[:, 2:3], CC[:, 0:1], -1.0, None, OP.mult))
                vneg = p.inc("v", "v", lambda e: e.drain())
                p.wait("s", "v", vneg)
                sv = p.inc("s", "s", lambda e: e.activation(EE[:], LT[:], AF.Exp, bias=CC[:, 2:3]))
                p.wait("v", "s", sv)
                vd(lambda e: e.tensor_reduce(CC[:, 3:4], EE[:], AX.X, OP.add))
                vd(lambda e: e.reciprocal(CC[:, 4:5], CC[:, 3:4]))
                vd(lambda e: e.tensor_tensor(CC[:, 5:6], CC[:, 1:2], CC[:, 0:1], OP.subtract))
                vsb = p.inc("v", "v", lambda e: e.drain())
                p.wait("s", "v", vsb)
                se = p.inc("s", "s", lambda e: e.activation(CC[:, 6:7], CC[:, 5:6], AF.Exp))
                p.wait("v", "s", se)
                vd(lambda e: e.tensor_tensor(CC[:, 7:8], CC[:, 6:7], CC[:, 4:5], OP.mult))
                vd(lambda e: e.tensor_tensor(CC[:, 8:9], CC[:, 4:5], CC[:, 7:8], OP.add))
                vd(lambda e: e.tensor_scalar(CC[:, 8:9], CC[:, 8:9], 1e-8, None, OP.add))
                vd(lambda e: e.reciprocal(CC[:, 9:10], CC[:, 8:9]))
                vd(lambda e: e.tensor_tensor(CC[:, 10:11], CC[:, 4:5], CC[:, 9:10], OP.mult))
                vd(lambda e: e.tensor_tensor(CC[:, 11:12], CC[:, 7:8], CC[:, 9:10], OP.mult))
            else:
                vd(lambda e: e.tensor_tensor(CC[:, 5:6], CC[:, 0:1], CC[:, 1:2], OP.subtract))
                vsb = p.inc("v", "v", lambda e: e.drain())
                p.wait("s", "v", vsb)
                sg = p.inc("s", "s", lambda e: e.activation(CC[:, 10:11], CC[:, 5:6], AF.Sigmoid))
                p.wait("v", "s", sg)
                p.run("v", lambda e: e.drain())
                vd(lambda e: e.tensor_scalar(CC[:, 11:12], CC[:, 10:11], -1.0, 1.0, OP.mult, OP.add))
            vd(lambda e: e.tensor_scalar(CC[:, 10:11], CC[:, 10:11], 0.5, None, OP.mult))
            vd(lambda e: e.tensor_scalar(CC[:, 11:12], CC[:, 11:12], 0.5, None, OP.mult))
            vd(lambda e: e.tensor_scalar(M2[:], M2[:], CC[:, 11:12], None, OP.mult))
            vd(lambda e: e.scalar_tensor_tensor(M1[:], M1[:], CC[:, 10:11], M2[:], OP.mult, OP.add))
            wv = p.inc("v", "v", lambda e: e.drain())
            p.wait("t", "v", wv)
            t2 = p.inc("t", "t", lambda e: e.transpose(PST[0:8, 8:136], M1[:], c["IDF"][:]))
            p.wait("v", "t", t2)
            p.inc("v", "v", lambda e, tsl=tsl: e.tensor_copy(WD[:, tsl], PST[0:8, 8:136]))
        p.wait_now("q", "v")
        p.dma("q", wd_dram[:, ch * CH:(ch + 1) * CH], WD)



def emit_moe_experts(p, tag, W1_dram, W2_dram, W3_dram, wd_dram, xlnh):
    c = p.c
    u = p.uid()
    E = 8
    DI = 2048
    NMO = 16
    WA = p.sbuf(f"wa{u}", [128, NMO * 8 * 128], BF16)
    WB = p.sbuf(f"wb{u}", [128, NMO * 8 * 128], BF16)
    WC = p.sbuf(f"wc{u}", [128, 8 * 16 * 128], BF16)
    FUS = p.sbuf(f"fus{u}", [128, NMO * CH], BF16)
    GS = p.sbuf(f"gs{u}", [128, CH], BF16)
    WBC = c["SCB"][:, 2 * CH:6 * CH]
    XC = c["XLNC"]
    PG = [c["PS0"], c["PS1"]]
    PV = [c["PS2"], c["PS3"]]
    PO = [c["PS4"], c["PS5"]]

    def load_weights(e):
        for (arena, wd) in ((WA, W1_dram), (WB, W2_dram)):
            src = wd[e].copy()
            s = src.rearrange("(k p) (mo j) -> p k mo j", p=128, j=128)
            d = arena[:].rearrange("p (k mo j) -> p k mo j", k=8, j=128)
            p.dma("g", d, s)
        src = W3_dram[e].copy()
        s = src.rearrange("(k p) (m j) -> p k m j", p=128, j=128)
        d = WC[:].rearrange("p (k m j) -> p k m j", k=16, j=128)
        return p.dma("g", d, s)

    wready = [None] * E
    wready[0] = load_weights(0)
    for e in range(E):
        p.wait("g", "v", p.val("v"))
        p.wait("g", "dq", p.val("dq"))
        wb = p.dma("g", WBC, wd_dram[e:e + 1, :].broadcast_to([128, T]))
        p.wait("t", "dg", wready[e])
        for ch in range(NCH):
            p.wait("q", "t", p.val("t"))
            p.wait("q", "dq", p.val("dq"))
            dk = None
            for i in range(NDT):
                dk = p.dma("q", XC[:, i * CH:(i + 1) * CH], xlnh[i * 128:(i + 1) * 128, ch * CH:(ch + 1) * CH])
            p.wait("t", "dq", dk)
            p.wait("v", "t", p.val("t"))
            for mo in range(NMO):
                b = mo % 2
                p.wait("t", "v", p.val("v"))
                p.wait("t", "s", p.val("s"))
                for k in range(NDT):
                    p.run("t", lambda e_, mo=mo, k=k, b=b: e_.matmul(PG[b][:, :], WA[:, (k * 16 + mo) * 128:(k * 16 + mo + 1) * 128], XC[:, k * CH:(k + 1) * CH], start=(k == 0), stop=(k == NDT - 1)))
                for k in range(NDT):
                    tv = p.inc("t", "t", lambda e_, mo=mo, k=k, b=b: e_.matmul(PV[b][:, :], WB[:, (k * 16 + mo) * 128:(k * 16 + mo + 1) * 128], XC[:, k * CH:(k + 1) * CH], start=(k == 0), stop=(k == NDT - 1)))
                p.wait("s", "t", tv)
                p.wait("s", "v", p.val("v"))
                sg = p.inc("s", "s", lambda e_, b=b: e_.activation(GS[:], PG[b][:, :], AF.Silu))
                p.wait("v", "dg", wb)
                p.wait("v", "t", tv)
                p.wait("v", "s", sg)
                p.run("v", lambda e_, b=b, ch=ch, mo=mo: e_.tensor_tensor(FUS[:, mo * CH:(mo + 1) * CH], PV[b][:, :], WBC[:, ch * CH:(ch + 1) * CH], OP.mult))
                p.inc("v", "v", lambda e_, mo=mo: e_.tensor_tensor(FUS[:, mo * CH:(mo + 1) * CH], FUS[:, mo * CH:(mo + 1) * CH], GS[:], OP.mult))
            if ch == NCH - 1 and e + 1 < E:
                p.wait("g", "t", p.val("t"))
                wready[e + 1] = load_weights(e + 1)
            fready = p.val("v")
            p.wait("t", "v", fready)
            for m in range(NDT):
                b = m % 2
                p.wait("t", "v", p.val("v"))
                for k2 in range(16):
                    to = p.inc("t", "t", lambda e_, m=m, k2=k2, b=b: e_.matmul(PO[b][:, :], WC[:, (k2 * 8 + m) * 128:(k2 * 8 + m + 1) * 128], FUS[:, k2 * CH:(k2 + 1) * CH], start=(k2 == 0), stop=(k2 == 15)))
                p.wait("v", "t", to)
                p.inc("v", "v", lambda e_, m=m, ch=ch, b=b: e_.tensor_tensor(c["X"][:, xslc(m, ch)], c["X"][:, xslc(m, ch)], PO[b][:, :], OP.add))


def emit_moe_sublayer(p, tag, mode, Wg_dram, W1_dram, W2_dram, W3_dram, xlnh, wd_dram, upto="full"):
    c = p.c
    u = p.uid()
    emit_ln_to_hbm(p, xlnh)
    if upto == "ln":
        return
    WGT = p.sbuf(f"wgt{u}", [128, 8 * NDT], BF16)
    if mode == "l0":
        for i in range(NDT):
            p.dma("g", WGT[:, i * 8:(i + 1) * 8], Wg_dram[i * 128:(i + 1) * 128, :])
    else:
        for i in range(NDT):
            src = Wg_dram[:, i * 128:(i + 1) * 128].transpose([1, 0])
            p.dma("g", WGT[:, i * 8:(i + 1) * 8], src, allow_slow_non_contiguous=True)
    gw = p.val("dg")
    p.wait("t", "dg", gw)
    emit_gate_top2(p, tag, WGT, wd_dram, mode, xlnh,
                   dbg=p.dbg_lg if hasattr(p, "dbg_lg") else None)
    if upto == "gate":
        return
    emit_moe_experts(p, tag, W1_dram, W2_dram, W3_dram, wd_dram, xlnh)



def build_l0f(which, upto="full"):
    p = Prog(which)
    p.upto = upto
    xin = p.inp("xin", [D, T], F32)
    xout = p.out("xout", [D, T], F32)
    if which == "l0f":
        Wg = p.inp("Wg", [1024, 8])
        W1 = p.inp("W1", [8, 1024, 2048])
        W2 = p.inp("W2", [8, 1024, 2048])
        W3 = p.inp("W3", [8, 2048, 1024])
        mode = "l0"
    else:
        Wg = p.inp("Wg", [8, 1024])
        W1 = p.inp("W1", [8, 1024, 2048])
        W2 = p.inp("W2", [8, 1024, 2048])
        W3 = p.inp("W3", [8, 2048, 1024])
        mode = "l3"
    xlnh = p.dram("xlnh", [D, T], BF16, kind="ExternalOutput" if upto in ("ln", "gate") else "Internal")
    if upto == "gate":
        p.dbg_lg = p.dram("dbg_lg", [8, T], F32, kind="ExternalOutput")
    wdd = p.dram("wdd", [8, T], F32, kind="ExternalOutput" if upto == "gate" else "Internal")
    setup_common(p)
    emit_load_x(p, xin)
    p.wait_now("v", "dq")
    emit_moe_sublayer(p, which, mode, Wg, W1, W2, W3, xlnh, wdd, upto=upto)
    emit_store_x(p, xout)
    p.wait("q", "dq", p.val("dq"))
    return p


def kernel(**inputs):
    from concourse.bass_utils import run_bass_kernel_spmd
    NC = 8
    g = inputs
    x = np.asarray(g["L0_attn_x"])
    xcur = [np.ascontiguousarray(x[c].T).astype(np.float32) for c in range(NC)]

    pms = [np.array([[1.0 if cc < c else 0.0] for cc in range(8)], np.float32)
           for c in range(NC)]

    def launch(p, permaps):
        maps = []
        for c in range(NC):
            mm = dict(permaps[c])
            for k, v in p.consts.items():
                if k not in mm:
                    mm[k] = v
            maps.append(mm)
        nc = p.finalize()
        res = run_bass_kernel_spmd(nc, maps, core_ids=list(range(NC)))
        return [np.asarray(res.results[c]["xout"]) for c in range(NC)]

    def wmap(d):
        return [dict(d, xin=xcur[c]) for c in range(NC)]

    m_l0f = {"Wg": g["L0_ffn_Wg"], "W1": g["L0_ffn_W11"], "W2": g["L0_ffn_W12"],
             "W3": g["L0_ffn_W2"]}
    m_l3f = {"Wg": g["L3_ffn_gate_w"], "W1": g["L3_ffn_w_gate"], "W2": g["L3_ffn_w_up"],
             "W3": g["L3_ffn_w_down"]}
    m_l0a = {}
    for pre in ("f", "b"):
        for nm in ("in_w", "conv_w", "conv_b", "dt_bias", "A_log", "Dp", "norm_w", "out_w"):
            m_l0a[f"{pre}_{nm}"] = g[f"L0_attn_{pre}_{nm}"]
    m_l1f = {"wg": g["L1_ffn_wg"], "W1": g["L1_ffn_W1"], "b1": g["L1_ffn_b1"],
             "W2": g["L1_ffn_W2"], "b2": g["L1_ffn_b2"]}
    m_l1a = {"Wq": g["L1_attn_Wq"], "Wk": g["L1_attn_Wk"], "Wv": g["L1_attn_Wv"],
             "Wf": g["L1_attn_Wf"], "gnw": g["L1_attn_g_norm_w"], "Wo": g["L1_attn_Wo"]}
    m_l2f = {"wqkv": g["L2_ffn_wqkv"], "q_scale": g["L2_ffn_q_scale"],
             "k_scale": g["L2_ffn_k_scale"], "wo": g["L2_ffn_wo"], "ln_g": g["L2_ffn_ln_g"],
             "ln_b": g["L2_ffn_ln_b"], "w1": g["L2_ffn_w1"], "b1": g["L2_ffn_b1"],
             "w2": g["L2_ffn_w2"], "b2": g["L2_ffn_b2"]}
    m_l2a = {"norm_w": g["L2_attn_norm_w"], "Win": g["L2_attn_Win"], "b_in": g["L2_attn_b_in"],
             "conv_w": g["L2_attn_conv_w"], "conv_b": g["L2_attn_conv_b"], "A": g["L2_attn_A"],
             "Bm": g["L2_attn_Bm"], "Cm": g["L2_attn_Cm"], "Wout": g["L2_attn_Wout"],
             "b_out": g["L2_attn_b_out"]}
    m_l3a = {"qkv": g["L3_attn_qkv"], "wo": g["L3_attn_wo"]}

    def launch_full(p, permaps, outs):
        maps = []
        for c in range(NC):
            mm = dict(permaps[c])
            for k, v in p.consts.items():
                if k not in mm:
                    mm[k] = v
            maps.append(mm)
        nc = p.finalize()
        res = run_bass_kernel_spmd(nc, maps, core_ids=list(range(NC)))
        return [{k: np.asarray(res.results[c][k]) for k in outs} for c in range(NC)]

    def do_l1f():
        pc = build_l1f_count()
        ra = launch_full(pc, [{"xin": xcur[c], "wg": g["L1_ffn_wg"]} for c in range(NC)],
                         ["xlnh", "oh8", "w1r", "c8", "cnt"])
        cnts = np.stack([ra[c]["cnt"][:, 0] for c in range(NC)], axis=0).astype(np.float32)
        pa = build_l1f_apply()
        permaps = []
        for c in range(NC):
            permaps.append(dict(m_l1f_w, xin=xcur[c], xlnh=ra[c]["xlnh"], oh8=ra[c]["oh8"],
                                w1r=ra[c]["w1r"], c8=ra[c]["c8"], cnts=cnts, pm=pms[c]))
        return launch(pa, permaps)

    m_l1f_w = {"W1": g["L1_ffn_W1"], "b1": g["L1_ffn_b1"],
               "W2": g["L1_ffn_W2"], "b2": g["L1_ffn_b2"]}
    steps = [
        (lambda: build_l0f("l0f"), m_l0f),
        (build_l0a, m_l0a),
        (lambda: build_l0f("l0f"), m_l0f),
        ("l1f", None),
        (build_l1a, m_l1a),
        ("l1f", None),
        (build_l2f, m_l2f),
        (build_l2a, m_l2a),
        (build_l2f, m_l2f),
        (lambda: build_l0f("l3f"), m_l3f),
        (build_l3a, m_l3a),
        (lambda: build_l0f("l3f"), m_l3f),
    ]
    for builder, wm in steps:
        if builder == "l1f":
            xcur = do_l1f()
        else:
            xcur = launch(builder(), wmap(wm))
    out = np.stack([np.ascontiguousarray(xcur[c].T) for c in range(NC)], axis=0)
    return out.astype(np.float32)



def emit_l1_gate(p, wg_dram, scale_dram, xlnh, pm_const, use_cc):
    c = p.c
    u = p.uid()
    CAP = 2561.0
    if use_cc == "ext":
        OH8, cm1 = p.sbuf_tmp(f"oh8{u}", [8, T], F32)
        C8, cm2 = p.sbuf_tmp(f"c8{u}", [8, T], F32)
        W1R, cm3 = p.sbuf_tmp(f"w1r{u}", [1, T], F32)
        p.gate_tmp_cms = [cm1, cm2, cm3]
        p.wait("q", "v", p.val("v"))
        p.wait("q", "s", p.val("s"))
        p.wait("q", "t", p.val("t"))
        p.dma("q", OH8[:], p.g_oh8_in)
        p.dma("q", W1R[:], p.g_w1r_in)
        lq = p.dma("q", C8[:], p.g_c8_in)
        p.wait("v", "dq", lq)
        p.wait("t", "dq", lq)
        sc = p.inc("v", "v", lambda e: e.drain())
        return _emit_l1_gate_apply(p, scale_dram, pm_const, use_cc, sc, OH8, C8, W1R, CAP, u)
    WGT, cmw = p.sbuf_tmp(f"wgt{u}", [128, 8 * NDT], BF16)
    for i in range(NDT):
        p.dma("g", WGT[:, i * 8:(i + 1) * 8], wg_dram[i * 128:(i + 1) * 128, :])
    gw = p.val("dg")
    p.wait("t", "dg", gw)
    PSG, PST, PSR = c["PS3"], c["PS4"], c["PS5"]
    LG = c["SCF"][0:8, 0:CH]
    OH8, cm1 = p.sbuf_tmp(f"oh8{u}", [8, T], F32)
    C8, cm2 = p.sbuf_tmp(f"c8{u}", [8, T], F32)
    W1R, cm3 = p.sbuf_tmp(f"w1r{u}", [1, T], F32)
    LT, cm4 = p.sbuf_tmp(f"lt{u}", [128, 8], F32)
    L2, cm5 = p.sbuf_tmp(f"l2{u}", [128, 8], F32)
    M1, cm6 = p.sbuf_tmp(f"m1{u}", [128, 8], F32)
    EE, cm7 = p.sbuf_tmp(f"ee{u}", [128, 8], F32)
    CC, cm8 = p.sbuf_tmp(f"cc{u}", [128, 16], F32)
    p.gate_tmp_cms = [cmw, cm1, cm2, cm3, cm4, cm5, cm6, cm7, cm8]
    XC = c["XLNC"]
    for ch in range(NCH):
        p.wait("q", "t", p.val("t"))
        p.wait("q", "v", p.val("v"))
        p.wait("q", "dq", p.val("dq"))
        for i in range(NDT):
            dk = p.dma("q", XC[:, i * CH:(i + 1) * CH], xlnh[i * 128:(i + 1) * 128, ch * CH:(ch + 1) * CH])
        p.wait("t", "dq", dk)
        p.wait("t", "v", p.val("v"))
        for i in range(NDT):
            tg = p.inc("t", "t", lambda e, i=i: e.matmul(PSG[0:8, :], WGT[:, i * 8:(i + 1) * 8], XC[:, i * CH:(i + 1) * CH], start=(i == 0), stop=(i == NDT - 1)))
        p.wait("v", "t", tg)
        ev = p.inc("v", "v", lambda e: e.tensor_copy(LG, PSG[0:8, :]))
        if getattr(p, "dbg_lg1", None) is not None:
            p.wait("q", "v", ev)
            p.dma("q", p.dbg_lg1[:, ch * CH:(ch + 1) * CH], LG)
        for q in range(4):
            tsl = slice(ch * CH + q * 128, ch * CH + (q + 1) * 128)
            p.wait("t", "v", p.val("v"))
            tr = p.inc("t", "t", lambda e, q=q: e.transpose(PST[:, 0:8], LG[:, q * 128:(q + 1) * 128], c["IDF"][0:8, 0:8]))
            p.wait("v", "t", tr)
            def vd(fn):
                p.run("v", fn)
                p.run("v", lambda e: e.drain())
            vd(lambda e: e.tensor_copy(LT[:], PST[0:128, 0:8]))
            if getattr(p, "dbg_lt", None) is not None and ch == 0 and q == 0:
                lc = p.inc("v", "v", lambda e: e.drain())
                p.wait("q", "v", lc)
                dd = p.dma("q", p.dbg_lt, LT[:])
                p.wait("v", "dq", dd)
            vd(lambda e: e.tensor_reduce(CC[:, 0:1], LT[:], AX.X, OP.max))
            vd(lambda e: e.tensor_scalar(M1[:], LT[:], CC[:, 0:1], None, OP.is_ge))
            vd(lambda e: e.tensor_scalar(CC[:, 2:3], CC[:, 0:1], -1.0, None, OP.mult))
            vneg = p.inc("v", "v", lambda e: e.drain())
            p.wait("s", "v", vneg)
            sv = p.inc("s", "s", lambda e: e.activation(EE[:], LT[:], AF.Exp, bias=CC[:, 2:3]))
            p.wait("v", "s", sv)
            vd(lambda e: e.tensor_reduce(CC[:, 3:4], EE[:], AX.X, OP.add))
            vd(lambda e: e.reciprocal(CC[:, 4:5], CC[:, 3:4]))
            vm = p.inc("v", "v", lambda e: e.drain())
            p.wait("t", "v", vm)
            t2 = p.inc("t", "t", lambda e: e.transpose(PST[0:8, 8:136], M1[:], c["IDF"][:]))
            p.wait("v", "t", t2)
            vd(lambda e, tsl=tsl: e.tensor_copy(OH8[:, tsl], PST[0:8, 8:136]))
            cwv = p.inc("v", "v", lambda e: e.drain())
            p.wait("t", "v", cwv)
            t3 = p.inc("t", "t", lambda e: e.transpose(PST[0:1, 136:264], CC[:, 4:5], c["IDF"][:]))
            p.wait("v", "t", t3)
            vd(lambda e, tsl=tsl: e.tensor_copy(W1R[0:1, tsl], PST[0:1, 136:264]))
            p.inc("v", "v", lambda e: e.drain())
    p.run("v", lambda e: e.tensor_copy(C8[:], OH8[:]))
    p.run("v", lambda e: e.drain())
    CS8, cm_cs8 = p.sbuf_tmp(f"cs8{u}", [8, T], F32)
    p.gate_tmp_cms.append(cm_cs8)
    emit_cumsum_free(p, C8, CS8, T)
    sc = p.inc("v", "v", lambda e: e.drain())
    if use_cc == "count":
        p.wait("q", "v", sc)
        p.dma("q", p.g_oh8, OH8[:])
        p.dma("q", p.g_w1r, W1R[:])
        p.dma("q", p.g_c8, C8[:])
        p.dma("q", p.g_cnt, C8[:, T - 1:T])
        p.wait("q", "dq", p.val("dq"))
        for cm in reversed(p.gate_tmp_cms):
            cm.__exit__(None, None, None)
        return
    _emit_l1_gate_apply(p, scale_dram, pm_const, use_cc, sc, OH8, C8, W1R, CAP, u)


def _emit_l1_gate_apply(p, scale_dram, pm_const, use_cc, sc, OH8, C8, W1R, CAP, u):
    c = p.c
    OFFS, cmo = p.sbuf_tmp(f"offs{u}", [8, 8], F32)
    p.gate_tmp_cms.append(cmo)
    if use_cc == "ext":
        ld = p.dma("g", OFFS[:], p.g_cnts_in)
        p.wait("t", "dg", ld)
    elif use_cc:
        cin = p.dram(f"ccin{u}", [8, 1], F32)
        cout = p.dram(f"ccout{u}", [64, 1], F32)
        p.wait("q", "v", sc)
        w1 = p.dma("q", cin, C8[:, T - 1:T])
        p.wait("g", "dq", w1)
        cc = p.inc("g", "dg", lambda e: e.collective_compute(
            "AllGather", OP.bypass, replica_groups=[list(range(NCORES))],
            ins=[cin], outs=[cout]), 16)
        p.wait("g", "dg", p.val("dg"))
        ld = p.dma("g", OFFS[:], cout.rearrange("(c e) one -> c (e one)", e=8))
        p.wait("t", "dg", ld)
    else:
        p.run("v", lambda e: e.memset(OFFS[:], 0.0))
        p.inc("v", "v", lambda e: e.drain())
        p.wait("t", "v", p.val("v"))
    PMS, cmp_ = p.sbuf_tmp(f"pms{u}", [8, 1], F32)
    p.gate_tmp_cms.append(cmp_)
    p.dma("q", PMS[:], pm_const)
    p.wait("t", "dq", p.val("dq"))
    p.wait("t", "v", sc)
    to = p.inc("t", "t", lambda e: e.matmul(c["PS5"][0:8, 0:1], OFFS[:], PMS[:], start=True, stop=True))
    p.wait("v", "t", to)
    OFC, cmf = p.sbuf_tmp(f"ofc{u}", [8, 1], F32)
    p.gate_tmp_cms.append(cmf)
    ofv = p.inc("v", "v", lambda e: e.tensor_copy(OFC[:], c["PS5"][0:8, 0:1]))
    p.run("v", lambda e: e.drain())
    rc = p.inc("v", "v", lambda e: e.tensor_tensor(C8[:], OH8[:], C8[:], OP.mult))
    p.wait("t", "v", rc)
    for ch in range(NCH):
        csl = slice(ch * CH, (ch + 1) * CH)
        p.wait("t", "v", p.val("v"))
        p.run("t", lambda e, csl=csl: e.matmul(c["PS3"][0:1, :], c["COL"][0:8, 1:2], C8[:, csl], start=True, stop=False))
        tr = p.inc("t", "t", lambda e, csl=csl: e.matmul(c["PS3"][0:1, :], OFC[:], OH8[:, csl], start=False, stop=True))
        p.wait("v", "t", tr)
        def vd2(fn):
            p.run("v", fn)
            p.run("v", lambda e: e.drain())
        vd2(lambda e, csl=csl: e.tensor_single_scalar(c["ROWS"][0:1, 0:CH], c["PS3"][0:1, :], CAP, OP.is_le))
        vd2(lambda e, csl=csl: e.tensor_tensor(W1R[0:1, csl], W1R[0:1, csl], c["ROWS"][0:1, 0:CH], OP.mult))
        vd2(lambda e, csl=csl: e.tensor_scalar(W1R[0:1, csl], W1R[0:1, csl], 0.5, None, OP.mult))
        wrd = p.inc("v", "v", lambda e: e.drain())
        p.wait("t", "v", wrd)
        tb = p.inc("t", "t", lambda e, csl=csl: e.matmul(c["PS4"][0:8, :], c["ON1F"][0:1, 0:8], W1R[0:1, csl], start=True, stop=True))
        p.wait("v", "t", tb)
        p.run("v", lambda e, csl=csl: e.tensor_tensor(OH8[:, csl], OH8[:, csl], c["PS4"][0:8, :], OP.mult))
        sv8 = p.inc("v", "v", lambda e: e.drain())
        p.wait("q", "v", sv8)
        p.dma("q", scale_dram[:, csl], OH8[:, csl])
    if getattr(p, "dbg_w1r", None) is not None:
        p.wait("q", "v", p.val("v"))
        p.dma("q", p.dbg_w1r, W1R[:])
        p.wait("q", "dq", p.val("dq"))
    for cm in reversed(p.gate_tmp_cms):
        cm.__exit__(None, None, None)


def emit_l1_experts(p, W1_dram, b1_dram, W2_dram, b2_dram, scale_dram, xlnh, xh, E=8, NCHL=NCH):
    c = p.c
    u = p.uid()
    NHO = 32
    WA = p.sbuf(f"l1wa{u}", [128, NHO * 8 * 128], BF16)
    WB = p.sbuf(f"l1wb{u}", [128, 8 * NHO * 128], BF16)
    HH = p.sbuf(f"l1h{u}", [128, NHO * CH], BF16)
    B1C = p.sbuf(f"l1b1{u}", [128, NHO], F32)
    B2C = p.sbuf(f"l1b2{u}", [128, NDT], F32)
    SBC = c["SCB"][:, CH:CH + T]
    XC = c["XLNC"]
    XK = c["XCK"]
    PG = [c["PS0"], c["PS1"]]
    PO = [c["PS2"], c["PS3"]]

    def load_weights(e):
        s = W1_dram[e].copy().rearrange("(k p) (ho j) -> p k ho j", p=128, j=128)
        d = WA[:].rearrange("p (k ho j) -> p k ho j", k=8, j=128)
        p.dma("g", d, s)
        s = W2_dram[e].copy().rearrange("(k p) (m j) -> p k m j", p=128, j=128)
        d = WB[:].rearrange("p (k m j) -> p k m j", k=NHO, j=128)
        p.dma("g", d, s)
        for ho in range(NHO):
            p.dma("g", B1C[:, ho:ho + 1], b1_dram[e, ho * 128:(ho + 1) * 128].unsqueeze(1))
        for m in range(NDT):
            p.dma("g", B2C[:, m:m + 1], b2_dram[e, m * 128:(m + 1) * 128].unsqueeze(1))
        return p.val("dg")

    wready = [None] * E
    p.wait("g", "v", p.val("v"))
    p.wait("g", "s", p.val("s"))
    p.wait("g", "t", p.val("t"))
    p.wait("g", "dq", p.val("dq"))
    wready[0] = load_weights(0)
    for e in range(E):
        p.wait("g", "v", p.val("v"))
        p.wait("g", "s", p.val("s"))
        p.wait("g", "dq", p.val("dq"))
        wb = p.dma("g", SBC[:], scale_dram[e:e + 1, :].broadcast_to([128, T]))
        p.wait("t", "dg", wready[e])
        p.wait("s", "dg", wready[e])
        for ch in range(NCHL):
            p.wait("q", "t", p.val("t"))
            p.wait("q", "v", p.val("v"))
            p.wait("q", "dq", p.val("dq"))
            for i in range(NDT):
                dk = p.dma("q", XC[:, i * CH:(i + 1) * CH], xlnh[i * 128:(i + 1) * 128, ch * CH:(ch + 1) * CH])
            p.wait("t", "dq", dk)
            for ho in range(NHO):
                b = ho % 2
                p.wait("t", "s", p.val("s"))
                for k in range(NDT):
                    tv = p.inc("t", "t", lambda e_, ho=ho, k=k, b=b: e_.matmul(PG[b][:, :], WA[:, (k * NHO + ho) * 128:(k * NHO + ho + 1) * 128], XC[:, k * CH:(k + 1) * CH], start=(k == 0), stop=(k == NDT - 1)))
                p.wait("s", "t", tv)
                p.inc("s", "s", lambda e_, ho=ho, b=b: e_.activation(HH[:, ho * CH:(ho + 1) * CH], PG[b][:, :], AF.Gelu, bias=B1C[:, ho:ho + 1]))
            p.wait("q", "v", p.val("v"))
            p.wait("q", "dq", p.val("dq"))
            xk = p.dma("q", XK[:].rearrange("p (i t) -> p i t", i=NDT),
                       xh.rearrange("(i p) tt -> p i tt", p=128)[:, :, ch * CH:(ch + 1) * CH])
            p.wait("v", "dq", xk)
            hready = p.val("s")
            p.wait("t", "s", hready)
            for m in range(NDT):
                b = m % 2
                p.wait("t", "v", p.val("v"))
                for k2 in range(NHO):
                    to = p.inc("t", "t", lambda e_, m=m, k2=k2, b=b: e_.matmul(PO[b][:, :], WB[:, (k2 * 8 + m) * 128:(k2 * 8 + m + 1) * 128], HH[:, k2 * CH:(k2 + 1) * CH], start=(k2 == 0), stop=(k2 == NHO - 1)))
                p.wait("v", "t", to)
                p.wait("v", "dg", wb)
                p.run("v", lambda e_, m=m, ch=ch, b=b: e_.scalar_tensor_tensor(c["SCB"][:, 0:CH], PO[b][:, :], B2C[:, m:m + 1], SBC[:, ch * CH:(ch + 1) * CH], OP.add, OP.mult))
                p.inc("v", "v", lambda e_, m=m, ch=ch: e_.tensor_tensor(XK[:, m * CH:(m + 1) * CH], XK[:, m * CH:(m + 1) * CH], c["SCB"][:, 0:CH], OP.add))
            if ch == NCH - 1 and e + 1 < E:
                p.wait("g", "t", p.val("t"))
                wready[e + 1] = load_weights(e + 1)
            xs = p.val("v")
            p.wait("q", "v", xs)
            p.dma("q", xh.rearrange("(i p) tt -> p i tt", p=128)[:, :, ch * CH:(ch + 1) * CH],
                  XK[:].rearrange("p (i t) -> p i t", i=NDT))


def build_l1f(use_cc=False, dbg=False):
    p = Prog("l1f")
    xin = p.inp("xin", [D, T], F32)
    xout = p.out("xout", [D, T], F32)
    wg = p.inp("wg", [1024, 8])
    W1 = p.inp("W1", [8, 1024, 4096])
    b1 = p.inp("b1", [8, 4096])
    W2 = p.inp("W2", [8, 4096, 1024])
    b2 = p.inp("b2", [8, 1024])
    pm = p.const("pm", np.zeros((8, 1), np.float32))
    xlnh = p.dram("xlnh", [D, T], BF16, kind="ExternalOutput" if dbg else "Internal")
    scale_dram = p.dram("sc8", [8, T], F32, kind="ExternalOutput" if dbg else "Internal")
    if dbg:
        p.dbg_lg1 = p.dram("dbg_lg1", [8, T], F32, kind="ExternalOutput")
        p.dbg_hh = p.dram("dbg_hh", [128, CH], BF16, kind="ExternalOutput")
        p.dbg_w1r = p.dram("dbg_w1r", [1, T], F32, kind="ExternalOutput")
        p.dbg_lt = p.dram("dbg_lt", [128, 8], F32, kind="ExternalOutput")
    xh = p.dram("xh", [D, T], F32)
    setup_common(p, need_x=False, need_xck=True)
    p.dma("q", xh, xin)
    p.wait("q", "dq", p.val("dq"))
    p.wait("v", "dq", p.val("dq"))
    p.wait("s", "dq", p.val("dq"))
    emit_ln_to_hbm(p, xlnh, xdram=xh)
    emit_l1_gate(p, wg, scale_dram, xlnh, pm, use_cc)
    emit_l1_experts(p, W1, b1, W2, b2, scale_dram, xlnh, xh,
                    E=getattr(p, "dbgE", 8), NCHL=getattr(p, "dbgNCH", NCH))
    if dbg:
        p.wait("q", "v", p.val("v")); p.wait("q", "s", p.val("s")); p.wait("q", "dq", p.val("dq"))
        p.dma("q", p.dbg_hh, p.c["XLNC"][:, 0:CH])
    p.wait("q", "dq", p.val("dq"))
    p.dma("q", xout, xh)
    p.wait("q", "dq", p.val("dq"))
    return p


def build_l1f_count():
    p = Prog("l1fc")
    xin = p.inp("xin", [D, T], F32)
    wg = p.inp("wg", [1024, 8])
    xlnh = p.dram("xlnh", [D, T], BF16, kind="ExternalOutput")
    p.g_oh8 = p.out("oh8", [8, T], F32)
    p.g_w1r = p.out("w1r", [1, T], F32)
    p.g_c8 = p.out("c8", [8, T], F32)
    p.g_cnt = p.out("cnt", [8, 1], F32)
    setup_common(p, need_x=False, need_xck=True)
    emit_ln_to_hbm(p, xlnh, xdram=xin)
    emit_l1_gate(p, wg, None, xlnh, None, use_cc="count")
    p.wait("q", "dq", p.val("dq"))
    return p


def build_l1f_apply():
    p = Prog("l1fa")
    xin = p.inp("xin", [D, T], F32)
    xout = p.out("xout", [D, T], F32)
    W1 = p.inp("W1", [8, 1024, 4096])
    b1 = p.inp("b1", [8, 4096])
    W2 = p.inp("W2", [8, 4096, 1024])
    b2 = p.inp("b2", [8, 1024])
    xlnh = p.inp("xlnh", [D, T], BF16)
    p.g_oh8_in = p.inp("oh8", [8, T], F32)
    p.g_w1r_in = p.inp("w1r", [1, T], F32)
    p.g_c8_in = p.inp("c8", [8, T], F32)
    p.g_cnts_in = p.inp("cnts", [8, 8], F32)
    pm = p.const("pm", np.zeros((8, 1), np.float32))
    scale_dram = p.dram("sc8", [8, T], F32)
    xh = p.dram("xh", [D, T], F32)
    setup_common(p, need_x=False, need_xck=True)
    p.dma("q", xh, xin)
    p.wait("q", "dq", p.val("dq"))
    p.wait("v", "dq", p.val("dq"))
    p.wait("s", "dq", p.val("dq"))
    emit_l1_gate(p, None, scale_dram, xlnh, pm, use_cc="ext")
    emit_l1_experts(p, W1, b1, W2, b2, scale_dram, xlnh, xh)
    p.wait("q", "dq", p.val("dq"))
    p.dma("q", xout, xh)
    p.wait("q", "dq", p.val("dq"))
    return p



def emit_l3_attn(p, qkv_dram, wo_dram, xlnh, xh):
    c = p.c
    u = p.uid()
    NH = 16
    XC = c["XLNC"]

    QS, cm_qs = p.sbuf_tmp(f"qs{u}", [128, 8 * T], BF16)
    KS, cm_ks = p.sbuf_tmp(f"ks{u}", [128, 8 * T], BF16)
    VS, cm_vs = p.sbuf_tmp(f"vs{u}", [128, 8 * T], BF16)
    WQT, cm_wqt = p.sbuf_tmp(f"wqt{u}", [128, 8 * 24 * 128], BF16)
    RAW, cm_raw = p.sbuf_tmp(f"raw{u}", [128, 4 * 1024], BF16)
    COSB, cm_cos = p.sbuf_tmp(f"cos{u}", [128, T], BF16)
    SINB, cm_sin = p.sbuf_tmp(f"sin{u}", [128, T], BF16)
    PROT, cm_prot = p.sbuf_tmp(f"prot{u}", [128, 128], BF16)
    QR = c["SCB"][:, 2 * CH:3 * CH]
    p.dma("q", COSB[:], p.l3_cos)
    p.dma("q", SINB[:], p.l3_sin)
    p.dma("q", PROT[:], p.l3_prot)
    PST = c["PSTB"]
    for g in range(6):
        dk = None
        for r in range(4):
            rt = g * 4 + r
            dk = p.dma("g", RAW[:, r * 1024:(r + 1) * 1024], qkv_dram[rt * 128:(rt + 1) * 128, :])
        p.wait("t", "dg", dk)
        p.wait("t", "v", p.val("v"))
        for r in range(4):
            rt = g * 4 + r
            for kc in range(8):
                p.wait("t", "v", p.val("v"))
                tt = p.inc("t", "t", lambda e, r=r, kc=kc: e.transpose(PST[:, 0:128], RAW[:, r * 1024 + kc * 128:r * 1024 + (kc + 1) * 128], c["IDB"][:]))
                p.wait("v", "t", tt)
                p.inc("v", "v", lambda e, rt=rt, kc=kc: e.tensor_copy(WQT[:, (kc * 24 + rt) * 128:(kc * 24 + rt + 1) * 128], PST[:, 0:128]))
        p.wait("g", "t", p.val("t"))
    PP = [c["PS0"], c["PS1"]]
    for ch in range(NCH):
        p.wait("q", "t", p.val("t"))
        p.wait("q", "dq", p.val("dq"))
        dk = None
        for i in range(NDT):
            dk = p.dma("q", XC[:, i * CH:(i + 1) * CH], xlnh[i * 128:(i + 1) * 128, ch * CH:(ch + 1) * CH])
        p.wait("t", "dq", dk)
        for ot in range(24):
            b = ot % 2
            p.wait("t", "v", p.val("v"))
            for k in range(NDT):
                tv = p.inc("t", "t", lambda e, ot=ot, k=k, b=b: e.matmul(PP[b][:, :], WQT[:, (k * 24 + ot) * 128:(k * 24 + ot + 1) * 128], XC[:, k * CH:(k + 1) * CH], start=(k == 0), stop=(k == NDT - 1)))
            p.wait("v", "t", tv)
            dst = [QS, KS, VS][ot // 8]
            j = ot % 8
            if ot // 8 == 2:
                p.inc("v", "v", lambda e, dst=dst, j=j, ch=ch, b=b: e.tensor_copy(dst[:, j * T + ch * CH:j * T + (ch + 1) * CH], PP[b][:, :]))
            else:
                ev = p.inc("v", "v", lambda e, b=b: e.tensor_copy(QR, PP[b][:, :]))
                p.wait("t", "v", ev)
                tr = p.inc("t", "t", lambda e: e.matmul(c["PS2"][:, :], PROT[:], QR, start=True, stop=True))
                p.wait("v", "t", tr)
                csl = slice(ch * CH, (ch + 1) * CH)
                p.run("v", lambda e, dst=dst, j=j, csl=csl: e.tensor_tensor(dst[:, j * T + csl.start:j * T + csl.stop], QR, COSB[:, csl], OP.mult))
                p.run("v", lambda e, csl=csl: e.scalar_tensor_tensor(c["SCB"][:, 3 * CH:4 * CH], c["PS2"][:, :], 1.0, SINB[:, csl], OP.mult, OP.mult))
                p.inc("v", "v", lambda e, dst=dst, j=j, csl=csl: e.tensor_tensor(dst[:, j * T + csl.start:j * T + csl.stop], dst[:, j * T + csl.start:j * T + csl.stop], c["SCB"][:, 3 * CH:4 * CH], OP.add))
    for cm in (cm_prot, cm_sin, cm_cos, cm_raw, cm_wqt):
        cm.__exit__(None, None, None)
    OB, cm_ob = p.sbuf_tmp(f"ob{u}", [128, 8 * T], BF16)
    SCR, cm_scr = p.sbuf_tmp(f"scr{u}", [128, T], F32)
    PB, cm_pb = p.sbuf_tmp(f"pb{u}", [128, T], BF16)
    PBT, cm_pbt = p.sbuf_tmp(f"pbt{u}", [128, 16 * 128], BF16)
    VT, cm_vt = p.sbuf_tmp(f"vt{u}", [128, 16 * 64], BF16)
    CCL, cm_ccl = p.sbuf_tmp(f"ccl{u}", [128, 4], F32)
    p.wait("t", "v", p.val("v"))
    p.wait("v", "t", p.val("t"))
    PSA = [c["PS3"], c["PS4"]]
    for h in range(NH):
        jt8 = h // 2
        rb = 64 * (h % 2)
        for jt in range(16):
            p.wait("t", "v", p.val("v"))
            vsl = VS[rb:rb + 64, jt8 * T + jt * 128: jt8 * T + (jt + 1) * 128]
            tt = p.inc("t", "t", lambda e, vsl=vsl, rb=rb: e.transpose(PST[:, 0:64], vsl, c["IDB"][rb:rb + 64, rb:rb + 64]))
            p.wait("v", "t", tt)
            p.inc("v", "v", lambda e, jt=jt: e.tensor_copy(VT[:, jt * 64:(jt + 1) * 64], PST[:, 0:64]))
        for it in range(16):
            nch = it // 4 + 1
            W = nch * 512
            for jc in range(nch):
                b = jc % 2
                p.wait("t", "v", p.val("v"))
                qap = QS[rb:rb + 64, jt8 * T + it * 128: jt8 * T + it * 128 + 128]
                kap = KS[rb:rb + 64, jt8 * T + jc * 512: jt8 * T + jc * 512 + 512]
                ts = p.inc("t", "t", lambda e, qap=qap, kap=kap, b=b: e.matmul(PSA[b][:, :], qap, kap, start=True, stop=True))
                p.wait("v", "t", ts)
                csl = slice(jc * 512, (jc + 1) * 512)
                if jc == it // 4:
                    th = float(it * 128 - jc * 512)
                    p.run("v", lambda e, th=th: e.tensor_single_scalar(PB[:, 0:CH], c["JMI"][:], th, OP.is_le))
                    p.run("v", lambda e, csl=csl, b=b: e.scalar_tensor_tensor(SCR[:, csl], PSA[b][:, :], 0.125, PB[:, 0:CH], OP.mult, OP.mult))
                else:
                    p.run("v", lambda e, csl=csl, b=b: e.tensor_scalar(SCR[:, csl], PSA[b][:, :], 0.125, None, OP.mult))
                p.run("v", lambda e, csl=csl: e.tensor_single_scalar(PB[:, 0:CH], SCR[:, csl], 0.0, OP.is_equal))
                p.inc("v", "v", lambda e, csl=csl: e.scalar_tensor_tensor(SCR[:, csl], PB[:, 0:CH], -1e30, SCR[:, csl], OP.mult, OP.add))
            p.run("v", lambda e, W=W: e.tensor_reduce(CCL[:, 0:1], SCR[:, 0:W], AX.X, OP.max))
            p.run("v", lambda e: e.drain())
            vr = p.inc("v", "v", lambda e: e.tensor_scalar(CCL[:, 1:2], CCL[:, 0:1], -1.0, None, OP.mult))
            p.wait("s", "v", vr)
            se = p.inc("s", "s", lambda e, W=W: e.activation(PB[:, 0:W], SCR[:, 0:W], AF.Exp, bias=CCL[:, 1:2]))
            p.wait("v", "s", se)
            p.run("v", lambda e, W=W: e.tensor_reduce(CCL[:, 2:3], PB[:, 0:W], AX.X, OP.add))
            p.run("v", lambda e: e.drain())
            p.run("v", lambda e: e.reciprocal(CCL[:, 3:4], CCL[:, 2:3]))
            p.run("v", lambda e: e.drain())
            vp = p.inc("v", "v", lambda e, W=W: e.tensor_scalar(PB[:, 0:W], PB[:, 0:W], CCL[:, 3:4], None, OP.mult))
            p.wait("t", "v", vp)
            for jt in range(nch * 4):
                p.wait("t", "v", p.val("v"))
                tt = p.inc("t", "t", lambda e, jt=jt: e.transpose(PST[:, 0:128], PB[:, jt * 128:(jt + 1) * 128], c["IDB"][:]))
                p.wait("v", "t", tt)
                p.inc("v", "v", lambda e, jt=jt: e.tensor_copy(PBT[:, jt * 128:(jt + 1) * 128], PST[:, 0:128]))
            p.wait("t", "v", p.val("v"))
            for jt in range(nch * 4):
                ta = p.inc("t", "t", lambda e, jt=jt, n=nch * 4: e.matmul(c["PS5"][0:64, 0:128], VT[:, jt * 64:(jt + 1) * 64], PBT[:, jt * 128:(jt + 1) * 128], start=(jt == 0), stop=(jt == n - 1)))
            p.wait("v", "t", ta)
            oap = OB[rb:rb + 64, jt8 * T + it * 128: jt8 * T + (it + 1) * 128]
            p.inc("v", "v", lambda e, oap=oap: e.tensor_copy(oap, c["PS5"][0:64, 0:128]))
    for cm in (cm_ccl, cm_vt, cm_pbt, cm_pb, cm_scr):
        cm.__exit__(None, None, None)
    WOT, cm_wot = p.sbuf_tmp(f"wot{u}", [128, 64 * 128], BF16)
    RAW2, cm_raw2 = p.sbuf_tmp(f"raw2{u}", [128, 4 * 1024], BF16)
    p.wait("g", "t", p.val("t"))
    p.wait("g", "v", p.val("v"))
    for g in range(2):
        dk = None
        for r in range(4):
            rt = g * 4 + r
            dk = p.dma("g", RAW2[:, r * 1024:(r + 1) * 1024], wo_dram[rt * 128:(rt + 1) * 128, :])
        p.wait("t", "dg", dk)
        p.wait("t", "v", p.val("v"))
        for r in range(4):
            rt = g * 4 + r
            for kc in range(8):
                p.wait("t", "v", p.val("v"))
                tt = p.inc("t", "t", lambda e, r=r, kc=kc: e.transpose(PST[:, 0:128], RAW2[:, r * 1024 + kc * 128:r * 1024 + (kc + 1) * 128], c["IDB"][:]))
                p.wait("v", "t", tt)
                p.inc("v", "v", lambda e, rt=rt, kc=kc: e.tensor_copy(WOT[:, (kc * 8 + rt) * 128:(kc * 8 + rt + 1) * 128], PST[:, 0:128]))
        p.wait("g", "t", p.val("t"))
    XK = c["XCK"]
    PO = [c["PS0"], c["PS1"]]
    for ch in range(NCH):
        p.wait("q", "v", p.val("v"))
        p.wait("q", "dq", p.val("dq"))
        xk = p.dma("q", XK[:].rearrange("p (i t) -> p i t", i=NDT),
                   xh.rearrange("(i p) tt -> p i tt", p=128)[:, :, ch * CH:(ch + 1) * CH])
        p.wait("v", "dq", xk)
        p.wait("t", "v", p.val("v"))
        for m in range(NDT):
            b = m % 2
            p.wait("t", "v", p.val("v"))
            for k in range(NDT):
                to = p.inc("t", "t", lambda e, m=m, k=k, ch=ch, b=b: e.matmul(PO[b][:, :], WOT[:, (k * 8 + m) * 128:(k * 8 + m + 1) * 128], OB[:, k * T + ch * CH:k * T + (ch + 1) * CH], start=(k == 0), stop=(k == NDT - 1)))
            p.wait("v", "t", to)
            p.inc("v", "v", lambda e, m=m, b=b: e.tensor_tensor(XK[:, m * CH:(m + 1) * CH], XK[:, m * CH:(m + 1) * CH], PO[b][:, :], OP.add))
        xs = p.val("v")
        p.wait("q", "v", xs)
        p.dma("q", xh.rearrange("(i p) tt -> p i tt", p=128)[:, :, ch * CH:(ch + 1) * CH],
              XK[:].rearrange("p (i t) -> p i t", i=NDT))
    for cm in (cm_raw2, cm_wot, cm_ob, cm_vs, cm_ks, cm_qs):
        cm.__exit__(None, None, None)


def build_l3a():
    p = Prog("l3a")
    xin = p.inp("xin", [D, T], F32)
    xout = p.out("xout", [D, T], F32)
    qkv = p.inp("qkv", [3072, 1024])
    wo = p.inp("wo", [1024, 1024])
    DH = 64
    inv = 1.0 / 10000.0 ** (np.arange(0, DH, 2, dtype=np.float64) / DH)
    ang = np.arange(T, dtype=np.float64)[:, None] * inv[None, :]
    cos1 = np.concatenate([np.cos(ang), np.cos(ang)], 1).T
    sin1 = np.concatenate([np.sin(ang), np.sin(ang)], 1).T
    cosb = np.concatenate([cos1, cos1], 0).astype(np.float32).astype(BF)
    sinb = np.concatenate([sin1, sin1], 0).astype(np.float32).astype(BF)
    prot = np.zeros((128, 128), np.float32)
    for hh in range(2):
        o = hh * 64
        for d in range(32):
            prot[o + d + 32, o + d] = -1.0
            prot[o + d, o + d + 32] = 1.0
    jmi = (np.arange(512)[None, :] - np.arange(128)[:, None]).astype(np.float32)
    p.l3_cos = p.const("l3cos", cosb)
    p.l3_sin = p.const("l3sin", sinb)
    p.l3_prot = p.const("l3prot", prot.astype(BF))
    jmic = p.const("l3jmi", jmi)
    xlnh = p.dram("xlnh", [D, T], BF16)
    xh = p.dram("xh", [D, T], F32)
    setup_common(p, need_x=False, need_xck=True)
    p.c["JMI"] = p.sbuf("JMI", [128, 512], F32)
    p.dma("q", p.c["JMI"][:], jmic)
    p.dma("q", xh, xin)
    p.wait("q", "dq", p.val("dq"))
    p.wait("v", "dq", p.val("dq"))
    p.wait("s", "dq", p.val("dq"))
    emit_ln_to_hbm(p, xlnh, xdram=xh)
    emit_l3_attn(p, qkv, wo, xlnh, xh)
    p.wait("q", "dq", p.val("dq"))
    p.dma("q", xout, xh)
    p.wait("q", "dq", p.val("dq"))
    return p



def emit_l1_attn(p, Wq, Wk, Wv, Wf, gnw, Wo, xlnh, xh):
    c = p.c
    u = p.uid()
    H = 4
    QC = 128
    NCK = T // QC
    PST = c["PSTB"]
    XC = c["XLNC"]

    OB, cm_ob = p.sbuf_tmp(f"gob{u}", [128, 8 * T], BF16)
    QF, cm_qf = p.sbuf_tmp(f"gqf{u}", [128, 8 * T], BF16)
    KF, cm_kf = p.sbuf_tmp(f"gkf{u}", [128, 8 * T], BF16)
    VF, cm_vf = p.sbuf_tmp(f"gvf{u}", [128, 8 * T], BF16)
    FF, cm_ff = p.sbuf_tmp(f"gff{u}", [128, 2 * T], BF16)
    WPT, cm_wpt = p.sbuf_tmp(f"gwpt{u}", [128, 8 * 8 * 128], BF16)
    HKB, cm_hk = p.sbuf_tmp(f"ghk{u}", [128, 8 * 64], F32)
    HKBB, cm_hkb = p.sbuf_tmp(f"ghkb{u}", [128, 8 * 64], BF16)
    HVB, cm_hv = p.sbuf_tmp(f"ghv{u}", [64, 4 * 256], F32)
    HVBB, cm_hvb = p.sbuf_tmp(f"ghvb{u}", [64, 4 * 256], BF16)
    GL, cm_gl = p.sbuf_tmp(f"ggl{u}", [128, 2 * QC], F32)
    EG, cm_eg = p.sbuf_tmp(f"geg{u}", [128, 2 * QC], F32)
    SGF, cm_sgf = p.sbuf_tmp(f"gsgf{u}", [128, 2 * QC], BF16)
    SC1, cm_sc1 = p.sbuf_tmp(f"gsc1{u}", [128, 2 * QC], F32)
    SGT, cm_sgt = p.sbuf_tmp(f"gsgt{u}", [128, 64], BF16)
    SGB0, cm_sgb0 = p.sbuf_tmp(f"gsgb0{u}", [64, QC], BF16)
    PF, cm_pf = p.sbuf_tmp(f"gpf{u}", [64, QC], BF16)
    PBM, cm_pbm = p.sbuf_tmp(f"gpbm{u}", [128, 64], BF16)
    KT, cm_kt = p.sbuf_tmp(f"gkt{u}", [128, 256], BF16)
    VT, cm_vt = p.sbuf_tmp(f"gvt{u}", [128, 256], BF16)
    WT, cm_wt = p.sbuf_tmp(f"gwt{u}", [128, QC], BF16)
    EGT, cm_egt = p.sbuf_tmp(f"gegt{u}", [128, 64], F32)
    LTM, cm_ltm = p.sbuf_tmp(f"gltm{u}", [128, 64], F32)
    CCg, cm_ccg = p.sbuf_tmp(f"gcc{u}", [128, 8], F32)
    DBS, cm_dbs = p.sbuf_tmp(f"gdbs{u}", [128, 64], F32)
    DRW, cm_drw = p.sbuf_tmp(f"gdrw{u}", [1, 64], F32)
    GNC, cm_gnc = p.sbuf_tmp(f"ggn{u}", [128, 2], F32)
    MASKJT, cm_mjt = p.sbuf_tmp(f"gmsk{u}", [128, QC], BF16)
    SQB, cm_sqb = p.sbuf_tmp(f"gsqb{u}", [128, 2 * QC], BF16)
    RW1, cm_rw1 = p.sbuf_tmp(f"grw1{u}", [1, QC], F32)
    p.dma("q", MASKJT[:], p.l1_mask)
    p.dma("q", GNC[:, 0:1], gnw[0:128].unsqueeze(1))
    p.dma("q", GNC[:, 1:2], gnw[128:256].unsqueeze(1))

    def proj(wdram, nout, dst, act, scale):
        s = wdram.copy().rearrange("(k p) (o j) -> p k o j", p=128, j=128)
        d = WPT[:, 0:8 * nout * 128].rearrange("p (k o j) -> p k o j", k=8, j=128)
        p.wait("g", "t", p.val("t"))
        wl = p.dma("g", d, s)
        p.wait("t", "dg", wl)
        for ch in range(NCH):
            p.wait("q", "t", p.val("t"))
            p.wait("q", "dq", p.val("dq"))
            for i in range(NDT):
                dk = p.dma("q", XC[:, i * CH:(i + 1) * CH], xlnh[i * 128:(i + 1) * 128, ch * CH:(ch + 1) * CH])
            p.wait("t", "dq", dk)
            for ot in range(nout):
                b = ot % 2
                p.wait("t", "s", p.val("s"))
                p.wait("t", "v", p.val("v"))
                for k in range(NDT):
                    tv = p.inc("t", "t", lambda e, ot=ot, k=k, b=b: e.matmul([c["PS0"], c["PS1"]][b][:, :], WPT[:, (k * nout + ot) * 128:(k * nout + ot + 1) * 128], XC[:, k * CH:(k + 1) * CH], start=(k == 0), stop=(k == NDT - 1)))
                dap = dst[:, ot * T + ch * CH: ot * T + (ch + 1) * CH]
                if act == "silu":
                    p.wait("s", "t", tv)
                    if scale == 1.0:
                        p.inc("s", "s", lambda e, dap=dap, b=b: e.activation(dap, [c["PS0"], c["PS1"]][b][:, :], AF.Silu))
                    else:
                        p.wait("s", "v", p.val("v"))
                        sv = p.inc("s", "s", lambda e, b=b: e.activation(c["SCB"][:, 0:CH], [c["PS0"], c["PS1"]][b][:, :], AF.Silu))
                        p.wait("v", "s", sv)
                        p.inc("v", "v", lambda e, dap=dap, scale=scale: e.tensor_scalar(dap, c["SCB"][:, 0:CH], scale, None, OP.mult))
                else:
                    p.wait("v", "t", tv)
                    p.inc("v", "v", lambda e, dap=dap, b=b: e.tensor_copy(dap, [c["PS0"], c["PS1"]][b][:, :]))

    proj(Wq, 8, QF, "silu", 0.0625)
    proj(Wk, 8, KF, "silu", 1.0)
    proj(Wv, 8, VF, None, 1.0)
    proj(Wf, 2, FF, None, 1.0)

    ONE1, cm_one1 = p.sbuf_tmp(f"gone{u}", [128, QC], F32)
    p.run("v", lambda e: e.memset(ONE1[:], 1.0))
    p.run("v", lambda e: e.memset(HKB[:], 0.0))
    p.run("v", lambda e: e.memset(HKBB[:], 0.0))
    p.run("v", lambda e: e.memset(HVB[:], 0.0))
    p.inc("v", "v", lambda e: e.memset(HVBB[:], 0.0))

    IDB = c["IDB"]
    for ck in range(NCK):
        t0 = ck * QC
        p.wait("v", "t", p.val("t"))
        for dt_ in range(2):
            fsl = FF[:, dt_ * T + t0:dt_ * T + t0 + QC]
            A_ = SC1[:, dt_ * QC:(dt_ + 1) * QC]
            G_ = GL[:, dt_ * QC:(dt_ + 1) * QC]
            E_ = EG[:, dt_ * QC:(dt_ + 1) * QC]
            S_ = SGF[:, dt_ * QC:(dt_ + 1) * QC]
            p.wait("s", "v", p.val("v"))
            p.run("s", lambda e, fsl=fsl, A_=A_: e.activation(A_, fsl, AF.Abs))
            p.run("s", lambda e, A_=A_: e.activation(A_, A_, AF.Exp, scale=c["COL"][:, 9:10]))
            sv = p.inc("s", "s", lambda e, A_=A_: e.activation(A_, A_, AF.Ln, bias=c["COL"][:, 1:2]))
            p.wait("v", "s", sv)
            p.run("v", lambda e, fsl=fsl, G_=G_: e.tensor_scalar(G_, fsl, -1.0, 0.0, OP.mult, OP.max))
            p.run("v", lambda e: e.drain())
            p.run("v", lambda e, A_=A_, G_=G_: e.tensor_tensor(A_, A_, G_, OP.add))
            p.run("v", lambda e: e.drain())
            vg = p.inc("v", "v", lambda e, A_=A_: e.tensor_scalar(A_, A_, -0.125, None, OP.mult))
            p.wait("s", "v", vg)
            se = p.inc("s", "s", lambda e, A_=A_, E_=E_: e.activation(E_, A_, AF.Exp))
            p.wait("v", "s", se)
            p.run("v", lambda e, E_=E_, S_=S_: e.tensor_scalar(S_, E_, -1.0, 1.0, OP.mult, OP.add))
            p.run("v", lambda e: e.drain())
            emit_cumsum_free(p, A_, G_, QC)
            p.run("v", lambda e, A_=A_, G_=G_: e.tensor_copy(G_, A_))
            vsc = p.inc("v", "v", lambda e: e.drain())
            p.wait("s", "v", vsc)
            s1 = p.inc("s", "s", lambda e, G_=G_, E_=E_: e.activation(E_, G_, AF.Exp))
            s2 = p.inc("s", "s", lambda e, G_=G_, A_=A_: e.activation(A_, G_, AF.Exp, scale=c["COL"][:, 9:10]))
            p.wait("v", "s", s2)
            p.run("v", lambda e, S_=S_, A_=A_: e.tensor_tensor(S_, S_, A_, OP.mult))
            p.inc("v", "v", lambda e: e.drain())
        for h in range(H):
            dt2 = h // 2
            rb = 64 * (h % 2)
            pr = h // 2
            idr = IDB[rb:rb + 64, rb:rb + 64]
            sgf_sl = SGF[rb:rb + 64, dt2 * QC:dt2 * QC + QC]
            eg_sl = EG[rb:rb + 64, dt2 * QC:dt2 * QC + QC]
            def tposeb(dst_ap, in_ap, ident, w, hgt=128):
                p.wait("t", "v", p.val("v"))
                tt = p.inc("t", "t", lambda e, in_ap=in_ap, ident=ident, w=w: e.transpose(PST[0:hgt, 0:w], in_ap, ident))
                p.wait("v", "t", tt)
                return p.inc("v", "v", lambda e, dst_ap=dst_ap, w=w, hgt=hgt: e.tensor_copy(dst_ap, PST[0:hgt, 0:w]))
            tposeb(SGT[:, 0:64], sgf_sl, idr, 64)
            p.inc("v", "v", lambda e, sgf_sl=sgf_sl: e.tensor_copy(SGB0[:, 0:QC], sgf_sl))
            p.wait("t", "v", p.val("v"))
            te = p.inc("t", "t", lambda e, eg_sl=eg_sl, rb=rb: e.transpose(c["PS5"][:, 0:64], eg_sl, c["IDF"][rb:rb + 64, rb:rb + 64]))
            p.wait("v", "t", te)
            p.inc("v", "v", lambda e: e.tensor_copy(EGT[:, 0:64], c["PS5"][0:128, 0:64]))
            for dt_ in range(2):
                kf_sl = KF[:, (2 * h + dt_) * T + t0:(2 * h + dt_) * T + t0 + QC]
                vf_sl = VF[:, (2 * h + dt_) * T + t0:(2 * h + dt_) * T + t0 + QC]
                tposeb(KT[:, dt_ * 128:(dt_ + 1) * 128], kf_sl, IDB[:], 128)
                tposeb(VT[:, dt_ * 128:(dt_ + 1) * 128], vf_sl, IDB[:], 128)
            p.wait("t", "v", p.val("v"))
            for dt_ in range(2):
                qsl_ = QF[:, (2 * h + dt_) * T + t0:(2 * h + dt_) * T + t0 + QC]
                ksl_ = KF[:, (2 * h + dt_) * T + t0:(2 * h + dt_) * T + t0 + QC]
                ta = p.inc("t", "t", lambda e, ksl_=ksl_, qsl_=qsl_, dt_=dt_: e.matmul(c["PS2"][:, 0:QC], ksl_, qsl_, start=(dt_ == 0), stop=(dt_ == 1)))
            p.wait("v", "t", ta)
            wt1 = p.inc("v", "v", lambda e: e.tensor_tensor(WT[:, 0:QC], c["PS2"][:, 0:QC], MASKJT[:], OP.mult))
            p.wait("t", "v", wt1)
            p.run("t", lambda e: e.matmul(c["PS3"][:, 0:64], WT[:, 0:QC], SGT[:, 0:64], start=True, stop=False))
            for dt_ in range(2):
                qsl_ = QF[:, (2 * h + dt_) * T + t0:(2 * h + dt_) * T + t0 + QC]
                hkb_ = HKBB[:, (h * 2 + dt_) * 64:(h * 2 + dt_ + 1) * 64]
                tl = p.inc("t", "t", lambda e, qsl_=qsl_, hkb_=hkb_, dt_=dt_: e.matmul(c["PS3"][:, 0:64], qsl_, hkb_, start=False, stop=(dt_ == 1)))
            p.wait("v", "t", tl)
            p.run("v", lambda e: e.tensor_tensor(LTM[:, 0:64], c["PS3"][0:128, 0:64], EGT[:, 0:64], OP.mult))
            p.run("v", lambda e: e.drain())
            p.run("v", lambda e: e.tensor_reduce(CCg[:, 0:1], LTM[:, 0:64], AX.X, OP.max))
            p.run("v", lambda e: e.drain())
            vm = p.inc("v", "v", lambda e: e.tensor_scalar(CCg[:, 1:2], CCg[:, 0:1], -1.0, None, OP.mult))
            p.wait("s", "v", vm)
            sx = p.inc("s", "s", lambda e: e.activation(LTM[:, 0:64], LTM[:, 0:64], AF.Exp, bias=CCg[:, 1:2]))
            p.wait("v", "s", sx)
            p.run("v", lambda e: e.tensor_reduce(CCg[:, 2:3], LTM[:, 0:64], AX.X, OP.add))
            p.run("v", lambda e: e.drain())
            p.run("v", lambda e: e.reciprocal(CCg[:, 3:4], CCg[:, 2:3]))
            p.run("v", lambda e: e.drain())
            p.run("v", lambda e: e.tensor_scalar(LTM[:, 0:64], LTM[:, 0:64], CCg[:, 3:4], None, OP.mult))
            vpp = p.inc("v", "v", lambda e: e.tensor_tensor(PBM[:, 0:64], LTM[:, 0:64], EGT[:, 0:64], OP.mult))
            p.wait("t", "v", vpp)
            tp = p.inc("t", "t", lambda e: e.transpose(PST[0:64, 0:128], PBM[:, 0:64], IDB[:]))
            p.wait("v", "t", tp)
            vpf = p.inc("v", "v", lambda e: e.tensor_copy(PF[0:64, 0:QC], PST[0:64, 0:128]))
            p.wait("t", "v", vpf)
            tw = p.inc("t", "t", lambda e: e.matmul(c["PS2"][:, 0:QC], SGB0[0:64, 0:QC], PF[0:64, 0:QC], start=True, stop=True))
            p.wait("v", "t", tw)
            wt2 = p.inc("v", "v", lambda e: e.tensor_tensor(WT[:, 0:QC], c["PS2"][:, 0:QC], MASKJT[:], OP.mult))
            p.wait("t", "v", wt2)
            for vt_ in range(2):
                p.run("t", lambda e, vt_=vt_: e.matmul(c["PS4"][:, vt_ * QC:(vt_ + 1) * QC], VT[:, vt_ * 128:(vt_ + 1) * 128], WT[:, 0:QC], start=True, stop=False))
                to = p.inc("t", "t", lambda e, vt_=vt_, h=h: e.matmul(c["PS4"][:, vt_ * QC:(vt_ + 1) * QC], HVBB[0:64, h * 256 + vt_ * 128:h * 256 + (vt_ + 1) * 128], PF[0:64, 0:QC], start=False, stop=True))
            p.wait("s", "t", to)
            sq1 = p.inc("s", "s", lambda e: e.activation(SQB[:, 0:2 * QC], c["PS4"][:, 0:2 * QC], AF.Square))
            p.wait("t", "s", sq1)
            for vt_ in range(2):
                tn = p.inc("t", "t", lambda e, vt_=vt_: e.matmul(c["PS5"][0:1, 0:QC], c["ONKB"][:], SQB[:, vt_ * QC:(vt_ + 1) * QC], start=(vt_ == 0), stop=(vt_ == 1)))
            p.wait("v", "t", tn)
            p.run("v", lambda e: e.tensor_scalar(RW1[0:1, 0:QC], c["PS5"][0:1, 0:QC], 1.0 / 256, None, OP.mult))
            p.run("v", lambda e: e.drain())
            vr = p.inc("v", "v", lambda e: e.drain())
            p.wait("s", "v", vr)
            p.run("s", lambda e: e.activation(RW1[0:1, 0:QC], RW1[0:1, 0:QC], AF.Ln, bias=c["COL"][0:1, 0:1]))
            sr = p.inc("s", "s", lambda e: e.activation(RW1[0:1, 0:QC], RW1[0:1, 0:QC], AF.Exp, scale=c["COL"][0:1, 8:9]))
            p.wait("t", "s", sr)
            tb = p.inc("t", "t", lambda e: e.matmul(c["PS5"][:, QC:2 * QC], c["ON1F"][:], RW1[0:1, 0:QC], start=True, stop=True))
            p.wait("v", "t", tb)
            for vt_ in range(2):
                osl = OB[:, (2 * h + vt_) * T + t0:(2 * h + vt_) * T + t0 + QC]
                p.run("v", lambda e, osl=osl, vt_=vt_: e.tensor_copy(osl, c["PS4"][:, vt_ * QC:(vt_ + 1) * QC]))
                p.inc("v", "v", lambda e, osl=osl, vt_=vt_: e.scalar_tensor_tensor(osl, osl, GNC[:, vt_:vt_ + 1], c["PS5"][:, QC:2 * QC], OP.mult, OP.mult))
            p.wait("t", "v", p.val("v"))
            td = p.inc("t", "t", lambda e, eg_sl=eg_sl, rb=rb: e.transpose(c["PS5"][0:1, 0:64], eg_sl[:, QC - 1:QC], c["IDF"][rb:rb + 64, rb:rb + 64]))
            p.wait("v", "t", td)
            vd1 = p.inc("v", "v", lambda e: e.tensor_copy(DRW[0:1, 0:64], c["PS5"][0:1, 0:64]))
            p.wait("t", "v", vd1)
            tdb = p.inc("t", "t", lambda e: e.matmul(c["PS5"][:, 64:128], c["ON1F"][:], DRW[0:1, 0:64], start=True, stop=True))
            p.wait("v", "t", tdb)
            vdb = p.inc("v", "v", lambda e: e.tensor_copy(DBS[:, 0:64], c["PS5"][0:128, 64:128]))
            for dt_ in range(2):
                p.wait("t", "v", p.val("v"))
                ti = p.inc("t", "t", lambda e, dt_=dt_: e.matmul(c["PS2"][:, 0:64], KT[:, dt_ * 128:(dt_ + 1) * 128], SGT[:, 0:64], start=True, stop=True))
                p.wait("v", "t", ti)
                hsl = HKB[:, (h * 2 + dt_) * 64:(h * 2 + dt_ + 1) * 64]
                hbl = HKBB[:, (h * 2 + dt_) * 64:(h * 2 + dt_ + 1) * 64]
                p.run("v", lambda e, hsl=hsl: e.tensor_tensor(hsl, hsl, c["PS2"][0:128, 0:64], OP.add))
                p.run("v", lambda e, hsl=hsl: e.tensor_tensor(hsl, hsl, DBS[:, 0:64], OP.mult))
                p.inc("v", "v", lambda e, hsl=hsl, hbl=hbl: e.tensor_copy(hbl, hsl))
            p.wait("t", "v", p.val("v"))
            tv2 = p.inc("t", "t", lambda e: e.matmul(c["PS3"][0:64, 0:256], SGT[:, 0:64], VT[:, 0:256], start=True, stop=True))
            p.wait("v", "t", tv2)
            hvl = HVB[0:64, h * 256:(h + 1) * 256]
            hvb = HVBB[0:64, h * 256:(h + 1) * 256]
            p.run("v", lambda e, hvl=hvl: e.tensor_tensor(hvl, hvl, c["PS3"][0:64, 0:256], OP.add))
            dcol = EG[rb:rb + 64, dt2 * QC + QC - 1:dt2 * QC + QC]
            p.run("v", lambda e, hvl=hvl, dcol=dcol: e.tensor_scalar(hvl, hvl, dcol, None, OP.mult))
            p.inc("v", "v", lambda e, hvl=hvl, hvb=hvb: e.tensor_copy(hvb, hvl))
    if getattr(p, "dbg_ob", None) is not None:
        p.wait("q", "v", p.val("v"))
        p.dma("q", p.dbg_ob, OB[:])
        p.wait("q", "dq", p.val("dq"))
    if getattr(p, "dbg_qf", None) is not None:
        p.wait("q", "v", p.val("v")); p.wait("q", "s", p.val("s"))
        p.dma("q", p.dbg_qf, QF[:])
        p.dma("q", p.dbg_ff, FF[:])
        p.wait("q", "dq", p.val("dq"))
    for cm in reversed([cm_qf, cm_kf, cm_vf, cm_ff, cm_wpt, cm_hk, cm_hkb, cm_hv, cm_hvb,
                        cm_gl, cm_eg, cm_sgf, cm_sc1, cm_sgt, cm_sgb0, cm_pf, cm_pbm, cm_kt, cm_vt,
                        cm_wt, cm_egt, cm_ltm, cm_ccg, cm_dbs, cm_drw, cm_gnc, cm_mjt, cm_sqb, cm_rw1, cm_one1]):
        cm.__exit__(None, None, None)
    WOT2, cm_wot2 = p.sbuf_tmp(f"gwot{u}", [128, 64 * 128], BF16)
    s = Wo.copy().rearrange("(k p) (m j) -> p k m j", p=128, j=128)
    d2 = WOT2[:].rearrange("p (k m j) -> p k m j", k=8, j=128)
    p.wait("g", "t", p.val("t"))
    p.wait("g", "v", p.val("v"))
    p.wait("g", "s", p.val("s"))
    wl = p.dma("g", d2, s)
    p.wait("t", "dg", wl)
    XK = c["XCK"]
    PO = [c["PS0"], c["PS1"]]
    for ch in range(NCH):
        p.wait("q", "v", p.val("v"))
        p.wait("q", "dq", p.val("dq"))
        xk = p.dma("q", XK[:].rearrange("p (i t) -> p i t", i=NDT),
                   xh.rearrange("(i p) tt -> p i tt", p=128)[:, :, ch * CH:(ch + 1) * CH])
        p.wait("v", "dq", xk)
        for m in range(NDT):
            b = m % 2
            p.wait("t", "v", p.val("v"))
            for k in range(NDT):
                to2 = p.inc("t", "t", lambda e, m=m, k=k, ch=ch, b=b: e.matmul(PO[b][:, :], WOT2[:, (k * 8 + m) * 128:(k * 8 + m + 1) * 128], OB[:, k * T + ch * CH:k * T + (ch + 1) * CH], start=(k == 0), stop=(k == NDT - 1)))
            p.wait("v", "t", to2)
            p.inc("v", "v", lambda e, m=m, b=b: e.tensor_tensor(XK[:, m * CH:(m + 1) * CH], XK[:, m * CH:(m + 1) * CH], PO[b][:, :], OP.add))
        xs = p.val("v")
        p.wait("q", "v", xs)
        p.dma("q", xh.rearrange("(i p) tt -> p i tt", p=128)[:, :, ch * CH:(ch + 1) * CH],
              XK[:].rearrange("p (i t) -> p i t", i=NDT))
    cm_wot2.__exit__(None, None, None)
    cm_ob.__exit__(None, None, None)


def build_l1a(dbg=False):
    p = Prog("l1a")
    if dbg:
        p.dbg_ob = p.out("dbg_ob", [128, 8 * T], BF16)
        p.dbg_qf = p.out("dbg_qf", [128, 8 * T], BF16)
        p.dbg_ff = p.out("dbg_ff", [128, 2 * T], BF16)
    xin = p.inp("xin", [D, T], F32)
    xout = p.out("xout", [D, T], F32)
    Wq = p.inp("Wq", [1024, 1024])
    Wk = p.inp("Wk", [1024, 1024])
    Wv = p.inp("Wv", [1024, 1024])
    Wf = p.inp("Wf", [1024, 256])
    gnw = p.inp("gnw", [256])
    Wo = p.inp("Wo", [1024, 1024])
    mask = (np.arange(128)[:, None] <= np.arange(128)[None, :]).astype(np.float32).astype(BF)
    p.l1_mask = p.const("l1mask", mask)
    xlnh = p.dram("xlnh", [D, T], BF16)
    xh = p.dram("xh", [D, T], F32)
    setup_common(p, need_x=False, need_xck=True, scb_cols=2 * CH, scf_cols=0)
    p.dma("q", xh, xin)
    p.wait("q", "dq", p.val("dq"))
    p.wait("v", "dq", p.val("dq"))
    p.wait("s", "dq", p.val("dq"))
    emit_ln_to_hbm(p, xlnh, xdram=xh)
    emit_l1_attn(p, Wq, Wk, Wv, Wf, gnw, Wo, xlnh, xh)
    p.wait("q", "dq", p.val("dq"))
    p.dma("q", xout, xh)
    p.wait("q", "dq", p.val("dq"))
    return p



def emit_l2_ffn(p, wqkv, q_scale, k_scale, wo, ln_g, ln_b, w1, b1, w2, b2, xlnh, xh):
    c = p.c
    u = p.uid()
    HD = 64
    NH = 8
    WIN = 512
    PST = c["PSTB"]
    X2, cm_x2 = p.sbuf_tmp(f"fx2{u}", [128, 8 * T], BF16)
    XS, cm_xs = p.sbuf_tmp(f"fxs{u}", [128, 8 * T], BF16)
    for i in range(NDT):
        dk = p.dma("q", c["XLNC"][:, i * CH:(i + 1) * CH], xlnh[i * 128:(i + 1) * 128, 0:CH])
    p.wait("v", "dq", p.val("dq"))
    for ch in range(NCH):
        p.wait("q", "v", p.val("v"))
        p.wait("q", "dq", p.val("dq"))
        for i in range(NDT):
            dk = p.dma("q", c["XLNC"][:, i * CH:(i + 1) * CH], xlnh[i * 128:(i + 1) * 128, ch * CH:(ch + 1) * CH])
        p.wait("v", "dq", dk)
        for i in range(NDT):
            p.run("v", lambda e, i=i, ch=ch: e.tensor_copy(X2[:, xslc(i, ch)], c["XLNC"][:, i * CH:(i + 1) * CH]))
        p.inc("v", "v", lambda e: e.drain())

    def shift_into_xs():
        p.wait("v", "t", p.val("t"))
        for i in range(NDT):
            if i < 4:
                p.run("v", lambda e, i=i: e.tensor_copy(XS[:, xslc(i)], X2[:, xslc(i)]))
            else:
                p.run("v", lambda e, i=i: e.tensor_copy(XS[:, i * T + 1:(i + 1) * T], X2[:, i * T:(i + 1) * T - 1]))
                p.run("v", lambda e, i=i: e.memset(XS[:, i * T:i * T + 1], 0.0))
        p.inc("v", "v", lambda e: e.drain())

    for l in range(2):
        shift_into_xs()
        QKV, cm_qkv = p.sbuf_tmp(f"fqkv{u}_{l}", [128, 12 * T], BF16)
        OF, cm_of = p.sbuf_tmp(f"fof{u}_{l}", [128, 4 * T], BF16)
        WQT, cm_wqt = p.sbuf_tmp(f"fwqt{u}_{l}", [128, 8 * 12 * 128], BF16)
        STGW, cm_stgw = p.sbuf_tmp(f"fstg{u}_{l}", [128, 1536], F32)
        p.wait("s", "v", p.val("v"))
        p.wait("q", "v", p.val("v"))
        p.wait("q", "t", p.val("t"))
        p.wait("q", "dg", p.val("dg"))
        wl = 0
        for k in range(8):
            p.wait("q", "s", p.val("s"))
            p.wait("q", "dq", p.val("dq"))
            dq1 = p.dma("q", STGW[:], wqkv[l][k * 128:(k + 1) * 128, :])
            p.wait("s", "dq", dq1)
            wl = p.inc("s", "s", lambda e, k=k: e.copy(WQT[:, k * 1536:(k + 1) * 1536], STGW[:]))
        cm_stgw.__exit__(None, None, None)
        p.wait("t", "s", wl)
        PP = [c["PS0"], c["PS1"]]
        for ch in range(NCH):
            for ot in range(12):
                b = ot % 2
                p.wait("t", "v", p.val("v"))
                for k in range(NDT):
                    tv = p.inc("t", "t", lambda e, ot=ot, k=k, b=b, ch=ch: e.matmul(PP[b][:, :], WQT[:, (k * 12 + ot) * 128:(k * 12 + ot + 1) * 128], XS[:, xslc(k, ch)], start=(k == 0), stop=(k == NDT - 1)))
                p.wait("v", "t", tv)
                p.inc("v", "v", lambda e, ot=ot, ch=ch, b=b: e.tensor_copy(QKV[:, ot * T + ch * CH:ot * T + (ch + 1) * CH], PP[b][:, :]))
        cm_wqt.__exit__(None, None, None)
        CSB, cm_csq = p.sbuf_tmp(f"fcsq{u}_{l}", [128, T], BF16)
        SNB, cm_snq = p.sbuf_tmp(f"fsnq{u}_{l}", [128, T], BF16)
        PROT2, cm_prot2 = p.sbuf_tmp(f"fprot{u}_{l}", [128, 128], BF16)
        SCQ, cm_scq = p.sbuf_tmp(f"fscq{u}_{l}", [128, 2], F32)
        SQT, cm_sqt = p.sbuf_tmp(f"fsqt{u}_{l}", [128, CH], BF16)
        p.wait("q", "v", p.val("v"))
        p.wait("q", "t", p.val("t"))
        p.wait("g", "v", p.val("v"))
        p.wait("g", "t", p.val("t"))
        d1 = p.dma("q", PROT2[:], p.l3_prot_ap)
        d1 = p.dma("g", SCQ[0:64, 0:1], q_scale[l].unsqueeze(1))
        d1 = p.dma("g", SCQ[64:128, 0:1], q_scale[l].unsqueeze(1))
        d1 = p.dma("g", SCQ[0:64, 1:2], k_scale[l].unsqueeze(1))
        d1 = p.dma("g", SCQ[64:128, 1:2], k_scale[l].unsqueeze(1))
        p.wait("t", "dq", p.val("dq"))
        p.wait("t", "dg", p.val("dg"))
        HSEL = c["HSEL"]
        B2SEL = c["B2SEL"]
        for qk in range(2):
            p.wait("q", "v", p.val("v"))
            p.wait("g", "v", p.val("v"))
            tq1 = p.dma("g", CSB[:], p.l2_cosq if qk == 0 else p.l2_cosk)
            tq2 = p.dma("g", SNB[:], p.l2_sinq if qk == 0 else p.l2_sink)
            p.wait("v", "dg", tq2)
            CS = CSB
            SN = SNB
            for tl in range(4):
                ti = qk * 4 + tl
                for ch in range(NCH):
                    qsl = QKV[:, ti * T + ch * CH: ti * T + (ch + 1) * CH]
                    p.wait("s", "v", p.val("v"))
                    sq = p.inc("s", "s", lambda e, qsl=qsl: e.activation(SQT[:], qsl, AF.Square))
                    p.wait("t", "s", sq)
                    p.wait("t", "v", p.val("v"))
                    t1 = p.inc("t", "t", lambda e: e.matmul(c["PS2"][0:2, :], HSEL[:], SQT[:], start=True, stop=True))
                    p.wait("v", "t", t1)
                    vv = p.inc("v", "v", lambda e: e.tensor_copy(c["ROWS"][0:2, 0:CH], c["PS2"][0:2, :]))
                    p.wait("s", "v", vv)
                    sv = p.inc("s", "s", lambda e: e.activation(c["ROWS"][0:2, 0:CH], c["ROWS"][0:2, 0:CH], AF.Sqrt))
                    p.wait("v", "s", sv)
                    p.run("v", lambda e: e.tensor_scalar(c["ROWS"][0:2, 0:CH], c["ROWS"][0:2, 0:CH], 1e-12, None, OP.max))
                    p.run("v", lambda e: e.drain())
                    rv = p.inc("v", "v", lambda e: e.reciprocal(c["ROWS"][0:2, CH:2 * CH], c["ROWS"][0:2, 0:CH]))
                    p.wait("t", "v", rv)
                    t2 = p.inc("t", "t", lambda e: e.matmul(c["PS2"][:, :], B2SEL[:], c["ROWS"][0:2, CH:2 * CH], start=True, stop=True))
                    p.wait("v", "t", t2)
                    vn = p.inc("v", "v", lambda e, qsl=qsl, qk=qk: e.scalar_tensor_tensor(qsl, qsl, SCQ[:, qk:qk + 1], c["PS2"][:, :], OP.mult, OP.mult))
                    p.wait("t", "v", p.val("v"))
                    t3 = p.inc("t", "t", lambda e, qsl=qsl: e.matmul(c["PS3"][:, :], PROT2[:], qsl, start=True, stop=True))
                    p.wait("v", "t", t3)
                    cssl = CS[:, ch * CH:(ch + 1) * CH]
                    snsl = SN[:, ch * CH:(ch + 1) * CH]
                    p.run("v", lambda e, qsl=qsl, cssl=cssl: e.tensor_tensor(c["SCB"][:, 0:CH], qsl, cssl, OP.mult))
                    p.run("v", lambda e, snsl=snsl: e.tensor_tensor(c["SCB"][:, CH:2 * CH], c["PS3"][:, :], snsl, OP.mult))
                    p.inc("v", "v", lambda e, qsl=qsl: e.tensor_tensor(qsl, c["SCB"][:, 0:CH], c["SCB"][:, CH:2 * CH], OP.add))
        for cm in (cm_sqt, cm_scq, cm_prot2, cm_snq, cm_csq):
            cm.__exit__(None, None, None)
        VTF, cm_vtf = p.sbuf_tmp(f"fvtf{u}_{l}", [128, 16 * 64], BF16)
        SCR2, cm_scr2 = p.sbuf_tmp(f"fscr{u}_{l}", [128, 1024], F32)
        PB2, cm_pb2 = p.sbuf_tmp(f"fpb{u}_{l}", [128, 1024], BF16)
        PBT2, cm_pbt2 = p.sbuf_tmp(f"fpbt{u}_{l}", [128, 8 * 128], BF16)
        NEGB, cm_negb = p.sbuf_tmp(f"fnegb{u}_{l}", [128, CH], F32)
        MQ8, cm_mq8 = p.sbuf_tmp(f"fmq8{u}_{l}", [128, CH], mybir.dt.uint8)
        CC2, cm_cc2 = p.sbuf_tmp(f"fcc{u}_{l}", [128, 4], F32)
        p.run("v", lambda e: e.memset(NEGB[:], -3.4e38))
        p.inc("v", "v", lambda e: e.drain())
        PSA = [c["PS3"], c["PS4"]]
        for h in range(NH):
            tq = h // 2
            rb = 64 * (h % 2)
            for jt in range(16):
                vsl = QKV[rb:rb + 64, (8 + tq) * T + jt * 128:(8 + tq) * T + (jt + 1) * 128]
                p.wait("t", "v", p.val("v"))
                tt = p.inc("t", "t", lambda e, vsl=vsl, rb=rb: e.transpose(PST[:, 0:64], vsl, c["IDB"][rb:rb + 64, rb:rb + 64]))
                p.wait("v", "t", tt)
                p.inc("v", "v", lambda e, jt=jt: e.tensor_copy(VTF[:, jt * 64:(jt + 1) * 64], PST[:, 0:64]))
            for w in range(4):
                for it in range(4):
                    i0 = w * WIN + it * 128
                    qap = QKV[rb:rb + 64, tq * T + i0: tq * T + i0 + 128]
                    blocks = []
                    if w > 0:
                        blocks.append(("prev", (w - 1) * WIN))
                    blocks.append(("own", w * WIN))
                    W = len(blocks) * WIN
                    for bi, (kind, j0) in enumerate(blocks):
                        kap = QKV[rb:rb + 64, (4 + tq) * T + j0: (4 + tq) * T + j0 + WIN]
                        bsel = bi % 2
                        p.wait("t", "v", p.val("v"))
                        ts = p.inc("t", "t", lambda e, qap=qap, kap=kap, bsel=bsel: e.matmul(PSA[bsel][:, :], qap, kap, start=True, stop=True))
                        p.wait("v", "t", ts)
                        th = float(128 * it) if kind == "own" else float(128 * it - 512)
                        opm = OP.is_le if kind == "own" else OP.is_ge
                        p.run("v", lambda e, th=th, opm=opm: e.tensor_single_scalar(MQ8[:], c["JMI5"][:], th, opm))
                        p.run("v", lambda e: e.drain())
                        ssl = SCR2[:, bi * WIN:(bi + 1) * WIN]
                        p.run("v", lambda e, ssl=ssl, bsel=bsel: e.select(ssl, MQ8[:], PSA[bsel][:, :], NEGB[:]))
                        p.inc("v", "v", lambda e: e.drain())
                    p.run("v", lambda e, W=W: e.tensor_reduce(CC2[:, 0:1], SCR2[:, 0:W], AX.X, OP.max))
                    p.run("v", lambda e: e.drain())
                    vr = p.inc("v", "v", lambda e: e.tensor_scalar(CC2[:, 1:2], CC2[:, 0:1], -1.0, None, OP.mult))
                    p.wait("s", "v", vr)
                    se = p.inc("s", "s", lambda e, W=W: e.activation(PB2[:, 0:W], SCR2[:, 0:W], AF.Exp, bias=CC2[:, 1:2]))
                    p.wait("v", "s", se)
                    p.run("v", lambda e, W=W: e.tensor_reduce(CC2[:, 2:3], PB2[:, 0:W], AX.X, OP.add))
                    p.run("v", lambda e: e.drain())
                    p.run("v", lambda e: e.reciprocal(CC2[:, 3:4], CC2[:, 2:3]))
                    p.run("v", lambda e: e.drain())
                    vp = p.inc("v", "v", lambda e, W=W: e.tensor_scalar(PB2[:, 0:W], PB2[:, 0:W], CC2[:, 3:4], None, OP.mult))
                    p.wait("t", "v", vp)
                    nj = W // 128
                    for jt in range(nj):
                        p.wait("t", "v", p.val("v"))
                        tt = p.inc("t", "t", lambda e, jt=jt: e.transpose(PST[:, 0:128], PB2[:, jt * 128:(jt + 1) * 128], c["IDB"][:]))
                        p.wait("v", "t", tt)
                        p.inc("v", "v", lambda e, jt=jt: e.tensor_copy(PBT2[:, jt * 128:(jt + 1) * 128], PST[:, 0:128]))
                    p.wait("t", "v", p.val("v"))
                    jbase = blocks[0][1] // 128
                    for jt in range(nj):
                        ta = p.inc("t", "t", lambda e, jt=jt, jbase=jbase, nj=nj: e.matmul(c["PS5"][0:64, 0:128], VTF[:, (jbase + jt) * 64:(jbase + jt + 1) * 64], PBT2[:, jt * 128:(jt + 1) * 128], start=(jt == 0), stop=(jt == nj - 1)))
                    p.wait("v", "t", ta)
                    oap = OF[rb:rb + 64, tq * T + i0: tq * T + i0 + 128]
                    p.inc("v", "v", lambda e, oap=oap: e.tensor_copy(oap, c["PS5"][0:64, 0:128]))
        for cm in (cm_cc2, cm_mq8, cm_negb, cm_pbt2, cm_pb2, cm_scr2, cm_vtf):
            cm.__exit__(None, None, None)
        WOT3, cm_wot3 = p.sbuf_tmp(f"fwot{u}_{l}", [128, 4 * 8 * 128], BF16)
        STG3, cm_stg3 = p.sbuf_tmp(f"fstg3{u}_{l}", [128, 1024], F32)
        p.wait("q", "v", p.val("v"))
        p.wait("q", "t", p.val("t"))
        p.wait("q", "dg", p.val("dg"))
        wl = 0
        for k in range(4):
            p.wait("q", "s", p.val("s"))
            p.wait("q", "dq", p.val("dq"))
            dq1 = p.dma("q", STG3[:], wo[l][k * 128:(k + 1) * 128, :])
            p.wait("s", "dq", dq1)
            wl = p.inc("s", "s", lambda e, k=k: e.copy(WOT3[:, k * 1024:(k + 1) * 1024], STG3[:]))
        cm_stg3.__exit__(None, None, None)
        p.wait("t", "s", wl)
        free_qkv_after_wo = True
        for ch in range(NCH):
            for m in range(NDT):
                b = m % 2
                p.wait("t", "v", p.val("v"))
                for k in range(4):
                    to = p.inc("t", "t", lambda e, m=m, k=k, ch=ch, b=b: e.matmul(PP[b][:, :], WOT3[:, (k * 8 + m) * 128:(k * 8 + m + 1) * 128], OF[:, k * T + ch * CH:k * T + (ch + 1) * CH], start=(k == 0), stop=(k == 3)))
                p.wait("v", "t", to)
                p.inc("v", "v", lambda e, m=m, ch=ch, b=b: e.tensor_tensor(X2[:, xslc(m, ch)], X2[:, xslc(m, ch)], PP[b][:, :], OP.add))
        cm_wot3.__exit__(None, None, None)
        cm_of.__exit__(None, None, None)
        cm_qkv.__exit__(None, None, None)
        shift_into_xs()
        GBC, cm_gbc = p.sbuf_tmp(f"fgbc{u}_{l}", [128, 16], F32)
        p.wait("g", "v", p.val("v"))
        p.wait("g", "t", p.val("t"))
        for i in range(NDT):
            p.dma("g", GBC[:, i:i + 1], ln_g[l, i * 128:(i + 1) * 128].unsqueeze(1))
            p.dma("g", GBC[:, 8 + i:9 + i], ln_b[l, i * 128:(i + 1) * 128].unsqueeze(1))
        gb = p.val("dg")
        p.wait("v", "dg", gb)
        emit_inner_ln(p, XS, GBC)
        NOT = 22
        H2, cm_h2 = p.sbuf_tmp(f"fh2{u}_{l}", [128, NOT * CH], BF16)
        WS, cm_ws = p.sbuf_tmp(f"fws{u}_{l}", [128, 5760], BF16)
        STGF, cm_stgf = p.sbuf_tmp(f"fstgf{u}_{l}", [128, 3072], F32)
        B1C2, cm_b1c2 = p.sbuf_tmp(f"fb1c{u}_{l}", [128, 2 * NOT], F32)
        B2C2, cm_b2c2 = p.sbuf_tmp(f"fb2c{u}_{l}", [128, 8], F32)
        p.wait("g", "v", p.val("v"))
        p.wait("g", "t", p.val("t"))
        p.wait("g", "s", p.val("s"))
        for ot in range(NOT):
            n = 128 if ot < NOT - 1 else 42
            p.dma("g", B1C2[0:n, ot:ot + 1], b1[l, ot * 128:ot * 128 + n].unsqueeze(1))
            p.dma("g", B1C2[0:n, NOT + ot:NOT + ot + 1], b1[l, 2730 + ot * 128:2730 + ot * 128 + n].unsqueeze(1))
        for m in range(NDT):
            p.dma("g", B2C2[:, m:m + 1], b2[l, m * 128:(m + 1) * 128].unsqueeze(1))
        bw = p.val("dg")
        p.wait("s", "dg", bw)
        p.wait("v", "dg", bw)
        for ch in range(NCH):
            for ot in range(NOT):
                n = 128 if ot < NOT - 1 else 42
                bsl = (ot % 2) * 2048
                p.wait("q", "s", p.val("s"))
                p.wait("q", "t", p.val("t"))
                p.wait("q", "dq", p.val("dq"))
                p.dma("q", STGF[:, 0:8 * n].rearrange("p (k j) -> p k j", k=8),
                      w1[l][:, ot * 128:ot * 128 + n].copy().rearrange("(k p) j -> p k j", p=128))
                dq1 = p.dma("q", STGF[:, 2048:2048 + 8 * n].rearrange("p (k j) -> p k j", k=8),
                            w1[l][:, 2730 + ot * 128:2730 + ot * 128 + n].copy().rearrange("(k p) j -> p k j", p=128))
                p.wait("s", "dq", dq1)
                p.run("s", lambda e, n=n, bsl=bsl: e.copy(WS[:, bsl:bsl + 8 * n], STGF[:, 0:8 * n]))
                wa = p.inc("s", "s", lambda e, n=n, bsl=bsl: e.copy(WS[:, bsl + 1024:bsl + 1024 + 8 * n], STGF[:, 2048:2048 + 8 * n]))
                p.wait("t", "s", wa)
                b = ot % 2
                p.wait("t", "v", p.val("v"))
                p.wait("t", "s", p.val("s"))
                for k in range(NDT):
                    p.run("t", lambda e, k=k, n=n, ch=ch, bsl=bsl: e.matmul(c["PS0"][0:n, :], WS[:, bsl + k * n:bsl + (k + 1) * n], XS[:, xslc(k, ch)], start=(k == 0), stop=(k == NDT - 1)))
                for k in range(NDT):
                    tg = p.inc("t", "t", lambda e, k=k, n=n, ch=ch, bsl=bsl: e.matmul(c["PS1"][0:n, :], WS[:, bsl + 1024 + k * n:bsl + 1024 + (k + 1) * n], XS[:, xslc(k, ch)], start=(k == 0), stop=(k == NDT - 1)))
                p.wait("s", "t", tg)
                sg = p.inc("s", "s", lambda e, n=n, ot=ot: e.activation(c["SCB"][0:n, 0:CH], c["PS1"][0:n, :], AF.Gelu, bias=B1C2[0:n, NOT + ot:NOT + ot + 1]))
                p.wait("v", "s", sg)
                p.run("v", lambda e, n=n, ot=ot: e.tensor_scalar(c["SCB"][0:n, CH:2 * CH], c["PS0"][0:n, :], B1C2[0:n, ot:ot + 1], None, OP.add))
                p.run("v", lambda e: e.drain())
                p.inc("v", "v", lambda e, n=n, ot=ot: e.tensor_tensor(H2[0:n, ot * CH:(ot + 1) * CH], c["SCB"][0:n, CH:2 * CH], c["SCB"][0:n, 0:CH], OP.mult))
            for m in range(NDT):
                bsl = (m % 2) * 2880
                p.wait("q", "s", p.val("s"))
                p.wait("q", "t", p.val("t"))
                p.wait("q", "dq", p.val("dq"))
                p.dma("q", STGF[:, 0:21 * 128].rearrange("p (k j) -> p k j", j=128),
                      w2[l][0:2688, m * 128:(m + 1) * 128].copy().rearrange("(k p) j -> p k j", p=128))
                dq1 = p.dma("q", STGF[0:42, 2688:2816], w2[l][2688:2730, m * 128:(m + 1) * 128])
                p.wait("s", "dq", dq1)
                p.run("s", lambda e, bsl=bsl: e.copy(WS[:, bsl:bsl + 21 * 128], STGF[:, 0:21 * 128]))
                w2t = p.inc("s", "s", lambda e, bsl=bsl: e.copy(WS[0:42, bsl + 21 * 128:bsl + 22 * 128], STGF[0:42, 2688:2816]))
                p.wait("t", "s", w2t)
                b = m % 2
                p.wait("t", "v", p.val("v"))
                for k2 in range(NOT):
                    n = 128 if k2 < NOT - 1 else 42
                    tm = p.inc("t", "t", lambda e, k2=k2, n=n, bsl=bsl, b=b: e.matmul([c["PS3"], c["PS4"]][b][:, :], WS[0:n, bsl + k2 * 128:bsl + k2 * 128 + 128], H2[0:n, k2 * CH:(k2 + 1) * CH], start=(k2 == 0), stop=(k2 == NOT - 1)))
                p.wait("v", "t", tm)
                p.inc("v", "v", lambda e, m=m, ch=ch, b=b: e.scalar_tensor_tensor(X2[:, xslc(m, ch)], [c["PS3"], c["PS4"]][b][:, :], B2C2[:, m:m + 1], X2[:, xslc(m, ch)], OP.add, OP.add))
        for cm in (cm_b2c2, cm_b1c2, cm_stgf, cm_ws, cm_h2, cm_gbc):
            cm.__exit__(None, None, None)
    XK = c["XCK"]
    for ch in range(NCH):
        p.wait("q", "v", p.val("v"))
        p.wait("q", "dq", p.val("dq"))
        xk = p.dma("q", XK[:].rearrange("p (i t) -> p i t", i=NDT),
                   xh.rearrange("(i p) tt -> p i tt", p=128)[:, :, ch * CH:(ch + 1) * CH])
        p.wait("v", "dq", xk)
        for i in range(NDT):
            p.run("v", lambda e, i=i, ch=ch: e.scalar_tensor_tensor(XK[:, i * CH:(i + 1) * CH], X2[:, xslc(i, ch)], 0.5, XK[:, i * CH:(i + 1) * CH], OP.mult, OP.add))
        xs2 = p.inc("v", "v", lambda e: e.drain())
        p.wait("q", "v", xs2)
        p.dma("q", xh.rearrange("(i p) tt -> p i tt", p=128)[:, :, ch * CH:(ch + 1) * CH],
              XK[:].rearrange("p (i t) -> p i t", i=NDT))
    for cm in (cm_xs, cm_x2):
        cm.__exit__(None, None, None)


def emit_inner_ln(p, XS, GBC):
    c = p.c
    ps_s, ps_q, ps_b = c["PS0"], c["PS1"], c["PS2"]
    mu = c["ROWS"][0:1, 0:CH]
    m2 = c["ROWS"][0:1, CH:2 * CH]
    tmp = c["ROWS"][0:1, 2 * CH:3 * CH]
    MUB = c["SCB"][:, 0:CH]
    RSB = c["SCB"][:, CH:2 * CH]
    rowread = 0
    p.wait("s", "v", p.val("v"))
    for ch in range(NCH):
        p.wait("t", "v", p.val("v"))
        p.wait("t", "s", p.val("s"))
        if ch > 0:
            p.wait("t", "v", rowread)
        for i in range(NDT):
            sq = p.inc("s", "s", lambda e, i=i, ch=ch: e.activation(c["XLNC"][:, i * CH:(i + 1) * CH], XS[:, xslc(i, ch)], AF.Square))
            p.wait("t", "s", sq)
            p.run("t", lambda e, i=i, ch=ch: e.matmul(ps_s[0:1, :], c["ONKB"][:], XS[:, xslc(i, ch)], start=(i == 0), stop=(i == NDT - 1)))
            tm = p.inc("t", "t", lambda e, i=i: e.matmul(ps_q[0:1, :], c["ONKB"][:], c["XLNC"][:, i * CH:(i + 1) * CH], start=(i == 0), stop=(i == NDT - 1)))
        p.wait("v", "t", tm)
        p.run("v", lambda e: e.tensor_scalar(mu, ps_s[0:1, :], 1.0 / D, None, OP.mult))
        p.run("v", lambda e: e.tensor_scalar(m2, ps_q[0:1, :], 1.0 / D, None, OP.mult))
        p.run("v", lambda e: e.drain())
        p.run("v", lambda e: e.tensor_tensor(tmp, mu, mu, OP.mult))
        p.run("v", lambda e: e.drain())
        vv = p.inc("v", "v", lambda e: e.tensor_tensor(m2, m2, tmp, OP.subtract))
        p.wait("s", "v", vv)
        p.run("s", lambda e: e.activation(m2, m2, AF.Ln, bias=c["COL"][0:1, 0:1]))
        sr = p.inc("s", "s", lambda e: e.activation(m2, m2, AF.Exp, scale=c["COL"][0:1, 8:9]))
        p.wait("t", "s", sr)
        tb = p.inc("t", "t", lambda e: e.matmul(ps_b[:, :], c["ON1F"][:], mu, start=True, stop=True))
        p.wait("v", "t", tb)
        ev = p.inc("v", "v", lambda e: e.tensor_copy(MUB, ps_b[:, :]))
        p.wait("t", "v", ev)
        tb2 = p.inc("t", "t", lambda e: e.matmul(ps_b[:, :], c["ON1F"][:], m2, start=True, stop=True))
        p.wait("v", "t", tb2)
        p.inc("v", "v", lambda e: e.tensor_copy(RSB, ps_b[:, :]))
        for i in range(NDT):
            p.run("v", lambda e, i=i, ch=ch: e.tensor_tensor(XS[:, xslc(i, ch)], XS[:, xslc(i, ch)], MUB, OP.subtract))
            p.run("v", lambda e: e.drain())
            p.run("v", lambda e, i=i, ch=ch: e.tensor_tensor(XS[:, xslc(i, ch)], XS[:, xslc(i, ch)], RSB, OP.mult))
            p.run("v", lambda e: e.drain())
            rowread = p.inc("v", "v", lambda e, i=i, ch=ch: e.tensor_scalar(XS[:, xslc(i, ch)], XS[:, xslc(i, ch)], GBC[:, i:i + 1], GBC[:, 8 + i:9 + i], OP.mult, OP.add))


def build_l2f():
    p = Prog("l2f")
    xin = p.inp("xin", [D, T], F32)
    xout = p.out("xout", [D, T], F32)
    wqkv = p.inp("wqkv", [2, 1024, 1536])
    q_scale = p.inp("q_scale", [2, 64])
    k_scale = p.inp("k_scale", [2, 64])
    wo = p.inp("wo", [2, 512, 1024])
    ln_g = p.inp("ln_g", [2, 1024])
    ln_b = p.inp("ln_b", [2, 1024])
    w1 = p.inp("w1", [2, 1024, 5460])
    b1 = p.inp("b1", [2, 5460])
    w2 = p.inp("w2", [2, 2730, 1024])
    b2 = p.inp("b2", [2, 1024])
    HD, WIN = 64, 512
    half = np.arange(0, HD, 2, dtype=np.float64)
    inv = 1.0 / 10000.0 ** (half / HD)
    t = np.arange(T, dtype=np.float64)
    fr = t[:, None] * inv[None, :]
    fr = np.concatenate([fr, fr], 1)
    basev = (half + 0.4 * HD) / (1.4 * HD)
    power = (t - T // 2) / (WIN // 2)
    scl = basev[None, :] ** power[:, None]
    scl = np.concatenate([scl, scl], 1)
    cos, sin = np.cos(fr), np.sin(fr)
    cq = (cos * scl).T; sq_ = (sin * scl).T
    ck = (cos / scl).T; sk = (sin / scl).T
    QKS = 8.0
    mk = lambda a, s=1.0: np.concatenate([a, a], 0).astype(np.float32) * s
    p.l2_cosq = p.const("l2cosq", mk(cq, QKS))
    p.l2_sinq = p.const("l2sinq", mk(sq_, QKS))
    p.l2_cosk = p.const("l2cosk", mk(ck))
    p.l2_sink = p.const("l2sink", mk(sk))
    prot = np.zeros((128, 128), np.float32)
    for hh in range(2):
        o = hh * 64
        for dd in range(32):
            prot[o + dd + 32, o + dd] = -1.0
            prot[o + dd, o + dd + 32] = 1.0
    p.l3_prot_ap = p.const("l2prot", prot.astype(BF))
    jmi5 = (np.arange(512)[None, :] - np.arange(128)[:, None]).astype(np.float32)
    jmic = p.const("l2jmi", jmi5)
    hsel = np.zeros((128, 2), np.float32)
    hsel[0:64, 0] = 1.0; hsel[64:128, 1] = 1.0
    hselc = p.const("l2hsel", hsel.astype(BF))
    b2sel = np.zeros((2, 128), np.float32)
    b2sel[0, 0:64] = 1.0; b2sel[1, 64:128] = 1.0
    b2selc = p.const("l2b2sel", b2sel)
    xlnh = p.dram("xlnh", [D, T], BF16)
    xh = p.dram("xh", [D, T], F32)
    setup_common(p, need_x=False, need_xck=True, scb_cols=2 * CH, scf_cols=0)
    c = p.c
    c["JMI5"] = p.sbuf("JMI5", [128, 512], F32)
    c["HSEL"] = p.sbuf("HSEL", [128, 2], BF16)
    c["B2SEL"] = p.sbuf("B2SEL", [2, 128], F32)
    p.dma("q", c["JMI5"][:], jmic)
    p.dma("q", c["HSEL"][:], hselc)
    p.dma("q", c["B2SEL"][:], b2selc)
    p.dma("q", xh, xin)
    p.wait("q", "dq", p.val("dq"))
    p.wait("v", "dq", p.val("dq"))
    p.wait("s", "dq", p.val("dq"))
    emit_ln_to_hbm(p, xlnh, xdram=xh)
    emit_l2_ffn(p, wqkv, q_scale, k_scale, wo, ln_g, ln_b, w1, b1, w2, b2, xlnh, xh)
    p.wait("q", "dq", p.val("dq"))
    p.dma("q", xout, xh)
    p.wait("q", "dq", p.val("dq"))
    return p



def emit_l2_attn(p, norm_w, Win, b_in, conv_w, conv_b, Am, Bm, Cm, Wout, b_out,
                 xlnh, xsrc, xdst):
    c = p.c
    u = p.uid()
    NTAP = 32
    DIN = 2048
    NO = 16
    PP = [c["PS0"], c["PS1"]]
    HB, cm_hb = p.sbuf_tmp(f"ahb{u}", [128, NDT * T], BF16)
    U, cm_u = p.sbuf_tmp(f"au{u}", [128, NO * T], BF16)
    NWC, cm_nwc = p.sbuf_tmp(f"anw{u}", [128, 8], F32)
    BIC, cm_bic = p.sbuf_tmp(f"abi{u}", [128, NO], F32)
    CWC, cm_cwc = p.sbuf_tmp(f"acw{u}", [128, 4 * NO], F32)
    CBC, cm_cbc = p.sbuf_tmp(f"acb{u}", [128, NO], F32)
    BOC, cm_boc = p.sbuf_tmp(f"abo{u}", [128, 8], F32)
    for i in range(NDT):
        p.dma("g", NWC[:, i:i + 1], norm_w[i * 128:(i + 1) * 128].unsqueeze(1))
        p.dma("g", BOC[:, i:i + 1], b_out[i * 128:(i + 1) * 128].unsqueeze(1))
    for o in range(NO):
        p.dma("g", BIC[:, o:o + 1], b_in[o * 128:(o + 1) * 128].unsqueeze(1))
        p.dma("g", CBC[:, o:o + 1], conv_b[o * 128:(o + 1) * 128].unsqueeze(1))
    p.dma("g", CWC[:].rearrange("p (o c) -> p o c", c=4),
          conv_w.squeeze(1).rearrange("(o p) c -> p o c", p=128))
    colw = p.val("dg")
    RSB = c["SCB"][:, 0:CH]
    m2 = c["ROWS"][0:1, 0:CH]
    SQ, cm_sq = p.sbuf_tmp(f"asq{u}", [128, NDT * CH], BF16)
    p.wait("v", "dg", colw)
    for ch in range(NCH):
        p.wait("q", "s", p.val("s"))
        p.wait("q", "v", p.val("v"))
        p.wait("q", "dq", p.val("dq"))
        xk = p.dma("q", c["XLNC"][:].rearrange("p (i t) -> p i t", i=NDT),
                   xlnh.rearrange("(i p) tt -> p i tt", p=128)[:, :, ch * CH:(ch + 1) * CH])
        p.wait("s", "dq", xk)
        sq = p.inc("s", "s", lambda e: e.activation(SQ[:], c["XLNC"][:], AF.Square))
        p.wait("t", "s", sq)
        p.wait("t", "v", p.val("v"))
        for i in range(NDT):
            tm = p.inc("t", "t", lambda e, i=i: e.matmul(c["PS2"][0:1, :], c["ONKB"][:], SQ[:, i * CH:(i + 1) * CH], start=(i == 0), stop=(i == NDT - 1)))
        p.wait("v", "t", tm)
        vv = p.inc("v", "v", lambda e: e.tensor_scalar(m2, c["PS2"][0:1, :], 1.0 / D, None, OP.mult))
        p.wait("s", "v", vv)
        p.run("s", lambda e: e.activation(m2, m2, AF.Ln, bias=c["COL"][0:1, 4:5]))
        sr = p.inc("s", "s", lambda e: e.activation(m2, m2, AF.Exp, scale=c["COL"][0:1, 8:9]))
        p.wait("t", "s", sr)
        tb = p.inc("t", "t", lambda e: e.matmul(c["PS3"][:, :], c["ON1F"][:], m2, start=True, stop=True))
        p.wait("v", "t", tb)
        p.inc("v", "v", lambda e: e.tensor_copy(RSB, c["PS3"][:, :]))
        for i in range(NDT):
            p.run("v", lambda e, i=i, ch=ch: e.scalar_tensor_tensor(HB[:, xslc(i, ch)], c["XLNC"][:, i * CH:(i + 1) * CH], NWC[:, i:i + 1], RSB, OP.mult, OP.mult))
        p.inc("v", "v", lambda e: e.drain())
    cm_sq.__exit__(None, None, None)
    WUB, cm_wub = p.sbuf_tmp(f"awu{u}", [128, 8 * DIN], BF16)
    STGW, cm_stgw = p.sbuf_tmp(f"astg{u}", [128, DIN], F32)
    p.wait("q", "v", p.val("v"))
    p.wait("q", "t", p.val("t"))
    p.wait("q", "dg", p.val("dg"))
    for k in range(NDT):
        p.wait("q", "s", p.val("s"))
        p.wait("q", "dq", p.val("dq"))
        dq1 = p.dma("q", STGW[:], Win[k * 128:(k + 1) * 128, :])
        p.wait("s", "dq", dq1)
        wl = p.inc("s", "s", lambda e, k=k: e.copy(WUB[:, k * DIN:(k + 1) * DIN], STGW[:]))
    p.wait("t", "s", wl)
    for ch in range(NCH):
        for o in range(NO):
            b = o % 2
            p.wait("t", "v", p.val("v"))
            for k in range(NDT):
                tv = p.inc("t", "t", lambda e, o=o, k=k, b=b, ch=ch: e.matmul(PP[b][:, :], WUB[:, k * DIN + o * 128:k * DIN + (o + 1) * 128], HB[:, xslc(k, ch)], start=(k == 0), stop=(k == NDT - 1)))
            p.wait("v", "t", tv)
            p.inc("v", "v", lambda e, o=o, ch=ch, b=b: e.tensor_scalar(U[:, o * T + ch * CH:o * T + (ch + 1) * CH], PP[b][:, :], BIC[:, o:o + 1], None, OP.add))
    cm_stgw.__exit__(None, None, None)
    cm_wub.__exit__(None, None, None)
    ACC, cm_acc = p.sbuf_tmp(f"aac{u}", [128, T], F32)
    CTB, cm_ctb = p.sbuf_tmp(f"act{u}", [128, NO * NTAP], F32)
    CWC3 = CWC[:].rearrange("p (o c) -> p o c", c=4)
    for o in range(NO):
        uo = U[:, o * T:(o + 1) * T]
        p.run("v", lambda e, uo=uo, o=o: e.tensor_scalar(ACC[:], uo, CWC3[:, o, 3:4], None, OP.mult))
        p.run("v", lambda e: e.drain())
        for j in range(1, 4):
            p.run("v", lambda e, uo=uo, o=o, j=j: e.scalar_tensor_tensor(ACC[:, j:T], uo[:, 0:T - j], CWC3[:, o, 3 - j:4 - j], ACC[:, j:T], OP.mult, OP.add))
            p.run("v", lambda e: e.drain())
        p.inc("v", "v", lambda e, uo=uo, o=o: e.tensor_scalar(uo, ACC[:], CBC[:, o:o + 1], None, OP.add))
    AT, cm_at = p.sbuf_tmp(f"aat{u}", [128, NO * 256], F32)
    WC, cm_wc = p.sbuf_tmp(f"awc{u}", [128, NO * 16], F32)
    WN, cm_wn = p.sbuf_tmp(f"awn{u}", [128, NO * 16], F32)
    BVt, cm_bv = p.sbuf_tmp(f"abv{u}", [128, NO * 16], F32)
    TMPC, cm_tmpc = p.sbuf_tmp(f"atc{u}", [128, NO * 16], F32)
    TMP4, cm_tmp4 = p.sbuf_tmp(f"at4{u}", [128, NO * 256], F32)
    p.wait("q", "v", p.val("v"))
    p.wait("q", "t", p.val("t"))
    p.wait("q", "s", p.val("s"))
    for gb in range(4):
        p.dma("q", AT[:, gb * 1024:(gb + 1) * 1024].rearrange("p (g s t) -> p g s t", s=16, t=16),
              Am[gb * 512:(gb + 1) * 512].rearrange("(g p) s t -> p g s t", p=128))
    p.dma("q", BVt[:].rearrange("p (g s) -> p g s", s=16),
          Bm.squeeze(2).rearrange("(g p) s -> p g s", p=128))
    dcv = p.dma("q", WC[:].rearrange("p (g s) -> p g s", s=16),
                Cm.squeeze(1).rearrange("(g p) s -> p g s", p=128))
    p.wait("v", "dq", dcv)
    AT4 = AT[:].rearrange("p (g s t) -> p g s t", s=16, t=16)
    CTB3 = CTB[:].rearrange("p (g r) -> p g r", r=NTAP)
    cur, nxt = WC, WN
    for r in range(NTAP):
        p.run("v", lambda e, cur=cur: e.tensor_tensor(TMPC[:], cur[:], BVt[:], OP.mult))
        p.run("v", lambda e: e.drain())
        p.run("v", lambda e, r=r: e.tensor_reduce(CTB3[:, :, r:r + 1], TMPC[:].rearrange("p (g s) -> p g s", s=16).unsqueeze(2), AX.X, OP.add))
        p.run("v", lambda e: e.drain())
        if r < NTAP - 1:
            wbc = cur[:].rearrange("p (g t) -> p g t", t=16).unsqueeze(2).broadcast_to([128, NO, 16, 16])
            p.run("v", lambda e, wbc=wbc: e.tensor_tensor(TMP4[:].rearrange("p (g s t) -> p g s t", s=16, t=16), AT4, wbc, OP.mult))
            p.run("v", lambda e: e.drain())
            p.run("v", lambda e, nxt=nxt: e.tensor_reduce(nxt[:].rearrange("p (g s) -> p g s", s=16), TMP4[:].rearrange("p (g s t) -> p g s t", s=16, t=16), AX.X, OP.add))
            p.run("v", lambda e: e.drain())
            cur, nxt = nxt, cur
    p.inc("v", "v", lambda e: e.drain())
    for cm in (cm_tmp4, cm_tmpc, cm_bv, cm_wn, cm_wc, cm_at):
        cm.__exit__(None, None, None)
    for o in range(NO):
        uo = U[:, o * T:(o + 1) * T]
        p.run("v", lambda e, uo=uo, o=o: e.tensor_scalar(ACC[:], uo, CTB3[:, o, 0:1], None, OP.mult))
        p.run("v", lambda e: e.drain())
        for r in range(1, NTAP):
            p.run("v", lambda e, uo=uo, o=o, r=r: e.scalar_tensor_tensor(ACC[:, r:T], uo[:, 0:T - r], CTB3[:, o, r:r + 1], ACC[:, r:T], OP.mult, OP.add))
            p.run("v", lambda e: e.drain())
        p.inc("v", "v", lambda e, uo=uo: e.tensor_copy(uo, ACC[:]))
    WOB, cm_wob = p.sbuf_tmp(f"awo{u}", [128, NO * D], BF16)
    STG2, cm_stg2 = p.sbuf_tmp(f"as2{u}", [128, D], F32)
    p.wait("q", "v", p.val("v"))
    p.wait("q", "t", p.val("t"))
    p.wait("q", "s", p.val("s"))
    for k2 in range(NO):
        p.wait("q", "s", p.val("s"))
        p.wait("q", "dq", p.val("dq"))
        dq1 = p.dma("q", STG2[:], Wout[k2 * 128:(k2 + 1) * 128, :])
        p.wait("s", "dq", dq1)
        w2l = p.inc("s", "s", lambda e, k2=k2: e.copy(WOB[:, k2 * D:(k2 + 1) * D], STG2[:]))
    p.wait("t", "s", w2l)
    XK = c["XCK"]
    for ch in range(NCH):
        p.wait("q", "v", p.val("v"))
        p.wait("q", "dq", p.val("dq"))
        xk = p.dma("q", XK[:].rearrange("p (i t) -> p i t", i=NDT),
                   xsrc.rearrange("(i p) tt -> p i tt", p=128)[:, :, ch * CH:(ch + 1) * CH])
        xl = p.dma("q", c["XLNC"][:].rearrange("p (i t) -> p i t", i=NDT),
                   xlnh.rearrange("(i p) tt -> p i tt", p=128)[:, :, ch * CH:(ch + 1) * CH])
        p.wait("v", "dq", xl)
        for f in range(NDT):
            b = f % 2
            p.wait("t", "v", p.val("v"))
            for k2 in range(NO):
                to = p.inc("t", "t", lambda e, f=f, k2=k2, b=b, ch=ch: e.matmul(PP[b][:, :], WOB[:, k2 * D + f * 128:k2 * D + (f + 1) * 128], U[:, k2 * T + ch * CH:k2 * T + (ch + 1) * CH], start=(k2 == 0), stop=(k2 == NO - 1)))
            p.wait("v", "t", to)
            p.run("v", lambda e, f=f, b=b: e.scalar_tensor_tensor(XK[:, f * CH:(f + 1) * CH], PP[b][:, :], BOC[:, f:f + 1], XK[:, f * CH:(f + 1) * CH], OP.add, OP.add))
            p.run("v", lambda e: e.drain())
            p.inc("v", "v", lambda e, f=f: e.tensor_tensor(XK[:, f * CH:(f + 1) * CH], XK[:, f * CH:(f + 1) * CH], c["XLNC"][:, f * CH:(f + 1) * CH], OP.add))
        xs2 = p.inc("v", "v", lambda e: e.drain())
        p.wait("q", "v", xs2)
        p.dma("q", xdst.rearrange("(i p) tt -> p i tt", p=128)[:, :, ch * CH:(ch + 1) * CH],
              XK[:].rearrange("p (i t) -> p i t", i=NDT))
    for cm in (cm_stg2, cm_wob, cm_ctb, cm_acc, cm_boc, cm_cbc, cm_cwc, cm_bic, cm_nwc, cm_u, cm_hb):
        cm.__exit__(None, None, None)



def revap(apx, n):
    r = apx.copy()
    r.ap[1] = [-1, n]
    r.offset = r.offset + (n - 1)
    return r


def qsl(o, q):
    return slice(o * T + q * 128, o * T + (q + 1) * 128)


def emit_l0a_dir(p, rev, in_w, conv_w, conv_b, dt_bias, A_log, Dp, norm_w, out_w,
                 XLNF, zh, xres_src, xres_dst):
    c = p.c
    u = p.uid()
    NH = 32
    NQ = 16
    NRO = 34
    PP = [c["PS0"], c["PS1"]]
    PST = c["PSTB"]
    XBC, cm_xbc = p.sbuf_tmp(f"mxbc{u}", [128, 17 * T], BF16)
    CVC, cm_cvc = p.sbuf_tmp(f"mcvc{u}", [128, 17 * 4], F32)
    CBC2, cm_cbc = p.sbuf_tmp(f"mcbc{u}", [128, 17], F32)
    DTBC, cm_dtb = p.sbuf_tmp(f"mdtb{u}", [32, 1], F32)
    ALC, cm_alc = p.sbuf_tmp(f"malc{u}", [32, 1], F32)
    NW2, cm_nw2 = p.sbuf_tmp(f"mnw{u}", [128, 16], F32)
    DPROW, cm_dpr = p.sbuf_tmp(f"mdpr{u}", [1, 32], F32)
    p.wait("g", "v", p.val("v"))
    p.wait("g", "t", p.val("t"))
    p.wait("g", "s", p.val("s"))
    p.dma("g", CVC[:].rearrange("p (o cc) -> p o cc", cc=4),
          conv_w.rearrange("(o p) cc -> p o cc", p=128))
    for o in range(17):
        p.dma("g", CBC2[:, o:o + 1], conv_b[o * 128:(o + 1) * 128].unsqueeze(1))
    p.dma("g", DTBC[:], dt_bias.unsqueeze(1))
    p.dma("g", ALC[:], A_log.unsqueeze(1))
    p.dma("g", DPROW[:], Dp.unsqueeze(0))
    for i in range(16):
        p.dma("g", NW2[:, i:i + 1], norm_w[i * 128:(i + 1) * 128].unsqueeze(1))
    colw = p.val("dg")
    CCH, cm_cch = p.sbuf_tmp(f"mcch{u}", [64, T], BF16)
    SS, cm_ss = p.sbuf_tmp(f"mss{u}", [64, NH * 64], F32)
    SSB, cm_ssb = p.sbuf_tmp(f"mssb{u}", [64, NH * 64], BF16)
    LAT, cm_lat = p.sbuf_tmp(f"mlat{u}", [128, NQ * NH], F32)
    RT, cm_rt = p.sbuf_tmp(f"mrt{u}", [128, NQ * NH], F32)
    EXT, cm_ext = p.sbuf_tmp(f"mext{u}", [128, NQ * NH], F32)
    WT, cm_wt = p.sbuf_tmp(f"mwt{u}", [128, NQ * NH], F32)
    DALT, cm_dal = p.sbuf_tmp(f"mdal{u}", [64, NQ * NH], F32)
    DPB, cm_dpb = p.sbuf_tmp(f"mdpb{u}", [128, NH], F32)
    LB, cm_lb = p.sbuf_tmp(f"mlb{u}", [1, NQ * NH], F32)
    LB1, cm_lb1 = p.sbuf_tmp(f"mlb1{u}", [1, 128], F32)
    EMF, cm_emf = p.sbuf_tmp(f"memf{u}", [128, 128], F32)
    EMF2, cm_emf2 = p.sbuf_tmp(f"memg{u}", [128, 128], F32)
    EMB, cm_emb = p.sbuf_tmp(f"memb{u}", [128, 128], BF16)
    SCMT, cm_scmt = p.sbuf_tmp(f"mscm{u}", [128, 128], BF16)
    XHT, cm_xht = p.sbuf_tmp(f"mxht{u}", [128, 64], BF16)
    BMT, cm_bmt = p.sbuf_tmp(f"mbmt{u}", [128, 64], BF16)
    BMW, cm_bmw = p.sbuf_tmp(f"mbmw{u}", [128, 64], BF16)
    YI, cm_yi = p.sbuf_tmp(f"myi{u}", [128, 64], BF16)
    YTOK, cm_ytok = p.sbuf_tmp(f"myt{u}", [128, 64], BF16)
    ACC, cm_acc = p.sbuf_tmp(f"macc{u}", [128, T], F32)
    DTF, cm_dtf = p.sbuf_tmp(f"mdtf{u}", [32, T], F32)
    LAB, cm_lab = p.sbuf_tmp(f"mlab{u}", [32, T], F32)
    DT2, cm_dt2 = p.sbuf_tmp(f"mdt2{u}", [32, T], F32)
    STGI, cm_stgi = p.sbuf_tmp(f"mstg{u}", [128, 1024], F32)
    WIT, cm_wit = p.sbuf_tmp(f"mwit{u}", [128, 8 * 128], BF16)
    XDIR = XLNF
    if rev:
        RTMP = ACC[:, 0:T // 2]
        HT = T // 2
        p.wait("v", "dq", p.val("dq"))
        p.wait("v", "t", p.val("t"))
        for i in range(NDT):
            sl = XLNF[:, xslc(i)]
            p.run("v", lambda e, sl=sl: e.tensor_copy(RTMP[:], revap(sl[:, HT:T], HT)))
            p.run("v", lambda e: e.drain())
            p.run("v", lambda e, sl=sl: e.tensor_copy(sl[:, HT:T], revap(sl[:, 0:HT], HT)))
            p.run("v", lambda e: e.drain())
            p.run("v", lambda e, sl=sl: e.tensor_copy(sl[:, 0:HT], RTMP[:]))
            p.run("v", lambda e: e.drain())
        p.inc("v", "v", lambda e: e.drain())
    p.wait("q", "v", p.val("v"))
    p.wait("q", "t", p.val("t"))
    p.wait("q", "s", p.val("s"))
    p.wait("q", "dg", p.val("dg"))
    p.wait("v", "dg", colw)
    zslot = 0
    for ro in range(NRO):
        nr = 128 if ro < NRO - 1 else 32
        p.wait("q", "t", p.val("t"))
        p.wait("q", "dq", p.val("dq"))
        dq1 = p.dma("q", STGI[0:nr, :], in_w[ro * 128:ro * 128 + nr, :])
        p.wait("t", "dq", dq1)
        for k in range(NDT):
            p.wait("t", "v", p.val("v"))
            tt = p.inc("t", "t", lambda e, k=k, nr=nr: e.transpose(c["PS5"][:, 0:nr], STGI[0:nr, k * 128:(k + 1) * 128], c["IDF"][0:nr, 0:nr]))
            p.wait("v", "t", tt)
            p.inc("v", "v", lambda e, k=k, nr=nr: e.tensor_copy(WIT[:, k * nr:(k + 1) * nr], c["PS5"][:, 0:nr]))
        for ch in range(NCH):
            b = ch % 2
            p.wait("t", "v", p.val("v"))
            p.wait("t", "s", p.val("s"))
            for k in range(NDT):
                tm = p.inc("t", "t", lambda e, k=k, ch=ch, b=b, nr=nr: e.matmul(PP[b][0:nr, :], WIT[:, k * nr:(k + 1) * nr], XDIR[:, xslc(k, ch)], start=(k == 0), stop=(k == NDT - 1)))
            if ro < 16:
                p.wait("s", "t", tm)
                p.wait("s", "dq", p.val("dq"))
                sz = p.inc("s", "s", lambda e, b=b, zslot=zslot: e.activation(c["SCB"][:, zslot * CH:(zslot + 1) * CH], PP[b][:, :], AF.Silu))
                p.wait("q", "s", sz)
                p.dma("q", zh[ro * 128:(ro + 1) * 128, ch * CH:(ch + 1) * CH], c["SCB"][:, zslot * CH:(zslot + 1) * CH])
                zslot = 1 - zslot
            elif ro < 33:
                p.wait("v", "t", tm)
                p.inc("v", "v", lambda e, ro=ro, ch=ch, b=b: e.tensor_copy(XBC[:, xslc(ro - 16, ch)], PP[b][:, :]))
            else:
                p.wait("v", "t", tm)
                p.inc("v", "v", lambda e, ch=ch, b=b: e.tensor_copy(DTF[:, ch * CH:(ch + 1) * CH], PP[b][0:32, :]))
    for cm in (cm_wit, cm_stgi):
        cm.__exit__(None, None, None)
    CVC3 = CVC[:].rearrange("p (o cc) -> p o cc", cc=4)
    p.wait("s", "v", p.val("v"))
    for o in range(17):
        uo = XBC[:, o * T:(o + 1) * T]
        p.wait("v", "s", p.val("s"))
        p.run("v", lambda e, uo=uo, o=o: e.tensor_scalar(ACC[:], uo, CVC3[:, o, 3:4], None, OP.mult))
        p.run("v", lambda e: e.drain())
        for j in range(1, 4):
            p.run("v", lambda e, uo=uo, o=o, j=j: e.scalar_tensor_tensor(ACC[:, j:T], uo[:, 0:T - j], CVC3[:, o, 3 - j:4 - j], ACC[:, j:T], OP.mult, OP.add))
            p.run("v", lambda e: e.drain())
        va = p.inc("v", "v", lambda e: e.drain())
        p.wait("s", "v", va)
        p.inc("s", "s", lambda e, uo=uo, o=o: e.activation(uo, ACC[:], AF.Silu, bias=CBC2[:, o:o + 1]))
    sconv = p.val("s")
    p.wait("v", "s", sconv)
    p.inc("v", "v", lambda e: e.tensor_copy(CCH[:], XBC[64:128, 16 * T:17 * T]))
    p.run("v", lambda e: e.tensor_scalar(DTF[:], DTF[:], DTBC[:], None, OP.add))
    vd = p.inc("v", "v", lambda e: e.drain())
    p.wait("s", "v", vd)
    p.run("s", lambda e: e.activation(DT2[:], DTF[:], AF.Abs))
    p.run("s", lambda e: e.activation(DT2[:], DT2[:], AF.Exp, scale=c["COL"][0:32, 9:10]))
    sp = p.inc("s", "s", lambda e: e.activation(DT2[:], DT2[:], AF.Ln, bias=c["COL"][0:32, 1:2]))
    p.wait("v", "s", sp)
    p.run("v", lambda e: e.tensor_scalar(LAB[:], DTF[:], 0.0, None, OP.max))
    p.run("v", lambda e: e.drain())
    p.run("v", lambda e: e.tensor_tensor(DTF[:], DT2[:], LAB[:], OP.add))
    vdt = p.inc("v", "v", lambda e: e.drain())
    p.wait("s", "v", vdt)
    sa = p.inc("s", "s", lambda e: e.activation(ALC[:], ALC[:], AF.Exp))
    p.wait("v", "s", sa)
    p.run("v", lambda e: e.tensor_scalar(ALC[:], ALC[:], -1.0, None, OP.mult))
    p.run("v", lambda e: e.drain())
    p.run("v", lambda e: e.tensor_scalar(LAB[:], DTF[:], ALC[:], None, OP.mult))
    p.inc("v", "v", lambda e: e.drain())
    for q in range(NQ):
        emit_cumsum_free(p, LAB[:, q * 128:(q + 1) * 128], DT2[:, q * 128:(q + 1) * 128], 128)
    vla = p.inc("v", "v", lambda e: e.drain())
    p.wait("s", "v", vla)
    sl = p.inc("s", "s", lambda e: e.activation(DT2[:], DTF[:], AF.Ln))
    p.wait("v", "s", sl)
    p.run("v", lambda e: e.tensor_tensor(DT2[:], DT2[:], LAB[:], OP.subtract))
    vrr = p.inc("v", "v", lambda e: e.drain())
    p.wait("t", "v", vrr)
    for q in range(NQ):
        t1 = p.inc("t", "t", lambda e, q=q: e.transpose(c["PS5"][:, 0:32], LAB[0:32, q * 128:(q + 1) * 128], c["IDF"][0:32, 0:32]))
        p.wait("v", "t", t1)
        p.inc("v", "v", lambda e, q=q: e.tensor_copy(LAT[:, q * 32:(q + 1) * 32], c["PS5"][:, 0:32]))
        p.wait("t", "v", p.val("v"))
        t2 = p.inc("t", "t", lambda e, q=q: e.transpose(c["PS5"][:, 32:64], DT2[0:32, q * 128:(q + 1) * 128], c["IDF"][0:32, 0:32]))
        p.wait("v", "t", t2)
        p.inc("v", "v", lambda e, q=q: e.tensor_copy(RT[:, q * 32:(q + 1) * 32], c["PS5"][:, 32:64]))
        p.wait("t", "v", p.val("v"))
    vtr = p.val("v")
    p.wait("s", "v", vtr)
    se = p.inc("s", "s", lambda e: e.activation(EXT[:], LAT[:], AF.Exp))
    p.wait("t", "v", vtr)
    tl = p.inc("t", "t", lambda e: e.matmul(c["PS5"][0:1, 0:512], p.e127_ap[:], LAT[:], start=True, stop=True))
    p.wait("v", "t", tl)
    vlb = p.inc("v", "v", lambda e: e.tensor_copy(LB[:], c["PS5"][0:1, 0:512]))
    p.wait("t", "v", vlb)
    tw = p.inc("t", "t", lambda e: e.matmul(c["PS4"][:, :], c["ON1F"][:], LB[:], start=True, stop=True))
    p.wait("v", "t", tw)
    vw = p.inc("v", "v", lambda e: e.tensor_tensor(WT[:], RT[:], c["PS4"][:, :], OP.add))
    p.wait("s", "v", vw)
    sw = p.inc("s", "s", lambda e: e.activation(WT[:], WT[:], AF.Exp))
    p.wait("s", "v", vlb)
    p.wait("s", "t", tw)
    slb = p.inc("s", "s", lambda e: e.activation(LB[:], LB[:], AF.Exp))
    p.wait("t", "s", slb)
    td = p.inc("t", "t", lambda e: e.matmul(c["PS3"][0:64, :], c["ON1F"][0:1, 0:64], LB[:], start=True, stop=True))
    p.wait("v", "t", td)
    p.inc("v", "v", lambda e: e.tensor_copy(DALT[:], c["PS3"][0:64, :]))
    p.wait("t", "v", p.val("v"))
    p.wait("t", "dg", colw)
    tp = p.inc("t", "t", lambda e: e.matmul(c["PS2"][:, 0:32], c["ON1F"][:], DPROW[:], start=True, stop=True))
    p.wait("v", "t", tp)
    p.inc("v", "v", lambda e: e.tensor_copy(DPB[:], c["PS2"][:, 0:32]))
    p.run("v", lambda e: e.memset(SS[:], 0.0))
    p.run("v", lambda e: e.memset(SSB[:], 0.0))
    p.inc("v", "v", lambda e: e.drain())
    for cm in (cm_dt2, cm_lab, cm_dtf, cm_acc):
        cm.__exit__(None, None, None)
    p.wait("s", "v", p.val("v"))
    p.wait("t", "s", p.val("s"))
    for q in range(NQ):
        p.wait("t", "v", p.val("v"))
        tb = p.inc("t", "t", lambda e, q=q: e.transpose(PST[:, 0:64], XBC[0:64, qsl(16, q)], c["IDB"][0:64, 0:64]))
        p.wait("v", "t", tb)
        p.inc("v", "v", lambda e: e.tensor_copy(BMT[:], PST[:, 0:64]))
        for h in range(NH):
            qh = q * 32 + h
            rb = (h % 2) * 64
            dt_ = h // 2
            p.wait("t", "v", p.val("v"))
            tx = p.inc("t", "t", lambda e, rb=rb, dt_=dt_, q=q: e.transpose(PST[:, 64:128], XBC[rb:rb + 64, qsl(dt_, q)], c["IDB"][rb:rb + 64, rb:rb + 64]))
            p.wait("v", "t", tx)
            vx = p.inc("v", "v", lambda e: e.tensor_copy(XHT[:], PST[:, 64:128]))
            p.wait("t", "v", vx)
            t1 = p.inc("t", "t", lambda e, qh=qh: e.transpose(c["PS5"][0:1, 0:128], LAT[:, qh:qh + 1], c["IDF"][:]))
            p.wait("v", "t", t1)
            v1 = p.inc("v", "v", lambda e: e.tensor_copy(LB1[:], c["PS5"][0:1, 0:128]))
            p.wait("t", "v", v1)
            t2 = p.inc("t", "t", lambda e: e.matmul(c["PS0"][:, 0:128], c["ON1F"][:], LB1[:], start=True, stop=True))
            p.wait("v", "t", t2)
            p.run("v", lambda e, qh=qh: e.tensor_scalar(EMF[:], c["PS0"][:, 0:128], RT[:, qh:qh + 1], None, OP.add))
            p.run("v", lambda e: e.drain())
            v2 = p.inc("v", "v", lambda e: e.tensor_tensor(EMF2[:], EMF[:], c["MASKA"][:], OP.add))
            p.wait("s", "v", v2)
            s1 = p.inc("s", "s", lambda e: e.activation(EMB[:], EMF2[:], AF.Exp))
            p.wait("t", "v", p.val("v"))
            t3 = p.inc("t", "t", lambda e, q=q: e.matmul(c["PS1"][:, 0:128], XBC[0:64, qsl(16, q)], CCH[:, q * 128:(q + 1) * 128], start=True, stop=True))
            p.wait("v", "t", t3)
            p.wait("v", "s", s1)
            v3 = p.inc("v", "v", lambda e: e.tensor_tensor(SCMT[:], EMB[:], c["PS1"][:, 0:128], OP.mult))
            p.wait("t", "v", v3)
            p.run("t", lambda e: e.matmul(c["PS2"][:, 0:64], SCMT[:], XHT[:], start=True, stop=True))
            t4 = p.inc("t", "t", lambda e, q=q, h=h: e.matmul(c["PS3"][:, 0:64], CCH[:, q * 128:(q + 1) * 128], SSB[:, h * 64:(h + 1) * 64], start=True, stop=True))
            p.wait("v", "t", t4)
            p.run("v", lambda e: e.tensor_copy(YI[:], c["PS3"][:, 0:64]))
            p.run("v", lambda e: e.drain())
            p.run("v", lambda e, qh=qh: e.scalar_tensor_tensor(YTOK[:], YI[:], EXT[:, qh:qh + 1], c["PS2"][:, 0:64], OP.mult, OP.add))
            p.run("v", lambda e: e.drain())
            p.run("v", lambda e, h=h: e.scalar_tensor_tensor(YTOK[:], XHT[:], DPB[:, h:h + 1], YTOK[:], OP.mult, OP.add))
            p.run("v", lambda e: e.drain())
            v4 = p.inc("v", "v", lambda e, qh=qh: e.tensor_scalar(BMW[:], BMT[:], WT[:, qh:qh + 1], None, OP.mult))
            p.wait("t", "v", v4)
            t5 = p.inc("t", "t", lambda e: e.matmul(c["PS4"][0:64, 0:64], BMW[:], XHT[:], start=True, stop=True))
            p.wait("v", "t", t5)
            p.run("v", lambda e, h=h, qh=qh: e.scalar_tensor_tensor(SS[:, h * 64:(h + 1) * 64], SS[:, h * 64:(h + 1) * 64], DALT[:, qh:qh + 1], c["PS4"][0:64, 0:64], OP.mult, OP.add))
            p.run("v", lambda e: e.drain())
            v5 = p.inc("v", "v", lambda e, h=h: e.tensor_copy(SSB[:, h * 64:(h + 1) * 64], SS[:, h * 64:(h + 1) * 64]))
            p.wait("t", "v", v5)
            t6 = p.inc("t", "t", lambda e: e.transpose(PST[0:64, 128:256], YTOK[:], c["IDB"][:]))
            p.wait("v", "t", t6)
            p.inc("v", "v", lambda e, rb=rb, dt_=dt_, q=q: e.tensor_copy(XBC[rb:rb + 64, qsl(dt_, q)], PST[0:64, 128:256]))
    ZCH, cm_zch = p.sbuf_tmp(f"mzc{u}", [128, 16 * CH], BF16)
    SQ2, cm_sq2 = p.sbuf_tmp(f"msq{u}", [128, 16 * CH], BF16)
    RSB = c["SCB"][:, 0:CH]
    m2 = c["ROWS"][0:1, 0:CH]
    for ch in range(NCH):
        p.wait("q", "v", p.val("v"))
        p.wait("q", "s", p.val("s"))
        p.wait("q", "dq", p.val("dq"))
        zk = p.dma("q", ZCH[:].rearrange("p (i t) -> p i t", i=16),
                   zh.rearrange("(i p) tt -> p i tt", p=128)[:, :, ch * CH:(ch + 1) * CH])
        p.wait("v", "dq", zk)
        for i in range(16):
            p.run("v", lambda e, i=i, ch=ch: e.tensor_tensor(XBC[:, xslc(i, ch)], XBC[:, xslc(i, ch)], ZCH[:, i * CH:(i + 1) * CH], OP.mult))
        vy = p.inc("v", "v", lambda e: e.drain())
        p.wait("s", "v", vy)
        for i in range(16):
            sq = p.inc("s", "s", lambda e, i=i, ch=ch: e.activation(SQ2[:, i * CH:(i + 1) * CH], XBC[:, xslc(i, ch)], AF.Square))
        p.wait("t", "s", sq)
        p.wait("t", "v", p.val("v"))
        for i in range(16):
            tm = p.inc("t", "t", lambda e, i=i: e.matmul(c["PS2"][0:1, :], c["ONKB"][:], SQ2[:, i * CH:(i + 1) * CH], start=(i == 0), stop=(i == 15)))
        p.wait("v", "t", tm)
        vv = p.inc("v", "v", lambda e: e.tensor_scalar(m2, c["PS2"][0:1, :], 1.0 / 2048.0, None, OP.mult))
        p.wait("s", "v", vv)
        p.run("s", lambda e: e.activation(m2, m2, AF.Ln, bias=c["COL"][0:1, 3:4]))
        sr = p.inc("s", "s", lambda e: e.activation(m2, m2, AF.Exp, scale=c["COL"][0:1, 8:9]))
        p.wait("t", "s", sr)
        tb = p.inc("t", "t", lambda e: e.matmul(c["PS3"][:, :], c["ON1F"][:], m2, start=True, stop=True))
        p.wait("v", "t", tb)
        p.inc("v", "v", lambda e: e.tensor_copy(RSB, c["PS3"][:, :]))
        for i in range(16):
            p.run("v", lambda e, i=i, ch=ch: e.scalar_tensor_tensor(XBC[:, xslc(i, ch)], XBC[:, xslc(i, ch)], NW2[:, i:i + 1], RSB, OP.mult, OP.mult))
        p.inc("v", "v", lambda e: e.drain())
    for cm in (cm_sq2, cm_zch):
        cm.__exit__(None, None, None)
    for cm in (cm_ytok, cm_yi, cm_bmw, cm_bmt, cm_xht, cm_scmt, cm_emb, cm_emf2, cm_emf,
               cm_lb1, cm_lb, cm_dpb, cm_dal, cm_wt, cm_ext,
               cm_rt, cm_lat, cm_ssb, cm_ss, cm_cch):
        cm.__exit__(None, None, None)
    WOTA, cm_wota = p.sbuf_tmp(f"mwo{u}", [128, 8 * 16 * 128], BF16)
    STGO, cm_stgo = p.sbuf_tmp(f"msto{u}", [128, 2048], F32)
    OTB, cm_otb = p.sbuf_tmp(f"motb{u}", [128, 2048], BF16)
    YRB, cm_yrb = p.sbuf_tmp(f"myr{u}", [128, CH], BF16)
    p.wait("q", "v", p.val("v"))
    p.wait("q", "t", p.val("t"))
    p.wait("q", "s", p.val("s"))
    p.wait("q", "dg", p.val("dg"))
    for fo in range(8):
        p.wait("q", "s", p.val("s"))
        p.wait("q", "t", p.val("t"))
        p.wait("q", "dq", p.val("dq"))
        dq1 = p.dma("q", STGO[:], out_w[fo * 128:(fo + 1) * 128, :])
        p.wait("s", "dq", dq1)
        so = p.inc("s", "s", lambda e: e.copy(OTB[:], STGO[:]))
        p.wait("t", "s", so)
        for k2 in range(16):
            p.wait("t", "v", p.val("v"))
            tt = p.inc("t", "t", lambda e, k2=k2: e.transpose(PST[:, 0:128], OTB[:, k2 * 128:(k2 + 1) * 128], c["IDB"][:]))
            p.wait("v", "t", tt)
            p.inc("v", "v", lambda e, fo=fo, k2=k2: e.tensor_copy(WOTA[:, (fo * 16 + k2) * 128:(fo * 16 + k2 + 1) * 128], PST[:, 0:128]))
    XK = c["XCK"]
    for och in range(NCH):
        pch = och if not rev else NCH - 1 - och
        p.wait("q", "v", p.val("v"))
        p.wait("q", "dq", p.val("dq"))
        xk = p.dma("q", XK[:].rearrange("p (i t) -> p i t", i=NDT),
                   xres_src.rearrange("(i p) tt -> p i tt", p=128)[:, :, och * CH:(och + 1) * CH])
        p.wait("v", "dq", xk)
        for fo in range(8):
            b = fo % 2
            p.wait("t", "v", p.val("v"))
            for k2 in range(16):
                to = p.inc("t", "t", lambda e, fo=fo, k2=k2, b=b, pch=pch: e.matmul(PP[b][:, :], WOTA[:, (fo * 16 + k2) * 128:(fo * 16 + k2 + 1) * 128], XBC[:, xslc(k2, pch)], start=(k2 == 0), stop=(k2 == 15)))
            p.wait("v", "t", to)
            if not rev:
                p.inc("v", "v", lambda e, fo=fo, b=b: e.tensor_tensor(XK[:, fo * CH:(fo + 1) * CH], XK[:, fo * CH:(fo + 1) * CH], PP[b][:, :], OP.add))
            else:
                p.run("v", lambda e, b=b: e.tensor_copy(YRB[:], PP[b][:, :]))
                p.run("v", lambda e: e.drain())
                p.inc("v", "v", lambda e, fo=fo: e.tensor_tensor(XK[:, fo * CH:(fo + 1) * CH], XK[:, fo * CH:(fo + 1) * CH], revap(YRB[:], CH), OP.add))
        xs2 = p.inc("v", "v", lambda e: e.drain())
        p.wait("q", "v", xs2)
        p.dma("q", xres_dst.rearrange("(i p) tt -> p i tt", p=128)[:, :, och * CH:(och + 1) * CH],
              XK[:].rearrange("p (i t) -> p i t", i=NDT))
    for cm in (cm_yrb, cm_otb, cm_stgo, cm_wota,
               cm_dpr, cm_nw2, cm_alc, cm_dtb, cm_cbc, cm_cvc, cm_xbc):
        cm.__exit__(None, None, None)


def build_l0a():
    p = Prog("l0a")
    xin = p.inp("xin", [D, T], F32)
    xout = p.out("xout", [D, T], F32)
    ins = {}
    for pre in ("f", "b"):
        ins[pre] = dict(
            in_w=p.inp(f"{pre}_in_w", [4256, 1024]),
            conv_w=p.inp(f"{pre}_conv_w", [2176, 4]),
            conv_b=p.inp(f"{pre}_conv_b", [2176]),
            dt_bias=p.inp(f"{pre}_dt_bias", [32]),
            A_log=p.inp(f"{pre}_A_log", [32]),
            Dp=p.inp(f"{pre}_Dp", [32]),
            norm_w=p.inp(f"{pre}_norm_w", [2048]),
            out_w=p.inp(f"{pre}_out_w", [1024, 2048]),
        )
    e127 = np.zeros((128, 1), np.float32); e127[127, 0] = 1.0
    e127c = p.const("e127", e127)
    maska = np.where(np.arange(128)[None, :] >= np.arange(128)[:, None], 0.0, -1e30).astype(np.float32)
    maskc = p.const("maska", maska)
    xlnh = p.dram("xlnh", [D, T], BF16)
    zh = p.dram("zh", [2048, T], BF16)
    setup_common(p, need_x=False, need_xck=True, scb_cols=2 * CH, scf_cols=0)
    c = p.c
    c["MASKA"] = p.sbuf("MASKA", [128, 128], F32)
    p.e127_ap = p.sbuf("E127", [128, 1], F32)
    p.dma("q", c["MASKA"][:], maskc)
    p.dma("q", p.e127_ap[:], e127c)
    emit_ln_to_hbm(p, xlnh, xdram=xin)
    XLNF = p.sbuf("XLNF", [128, NDT * T], BF16)
    p.wait("q", "dq", p.val("dq"))
    p.wait("q", "v", p.val("v"))
    p.wait("q", "s", p.val("s"))
    for i in range(NDT):
        p.dma("q", XLNF[:, xslc(i)], xlnh[i * 128:(i + 1) * 128, :])
    p.wait("v", "dq", p.val("dq"))
    p.wait("t", "dq", p.val("dq"))
    emit_l0a_dir(p, False, xres_src=xin, xres_dst=xout, XLNF=XLNF, zh=zh, **ins["f"])
    emit_l0a_dir(p, True, xres_src=xout, xres_dst=xout, XLNF=XLNF, zh=zh, **ins["b"])
    p.wait("q", "dq", p.val("dq"))
    return p


def build_l2a():
    p = Prog("l2a")
    xin = p.inp("xin", [D, T], F32)
    xout = p.out("xout", [D, T], F32)
    norm_w = p.inp("norm_w", [1024])
    Win = p.inp("Win", [1024, 2048])
    b_in = p.inp("b_in", [2048])
    conv_w = p.inp("conv_w", [2048, 1, 4])
    conv_b = p.inp("conv_b", [2048])
    Am = p.inp("A", [2048, 16, 16])
    Bm = p.inp("Bm", [2048, 16, 1])
    Cm = p.inp("Cm", [2048, 1, 16])
    Wout = p.inp("Wout", [2048, 1024])
    b_out = p.inp("b_out", [1024])
    xlnh = p.dram("xlnh", [D, T], BF16)
    setup_common(p, need_x=False, need_xck=True, scb_cols=2 * CH, scf_cols=0)
    emit_ln_to_hbm(p, xlnh, xdram=xin)
    emit_l2_attn(p, norm_w, Win, b_in, conv_w, conv_b, Am, Bm, Cm, Wout, b_out,
                 xlnh, xin, xout)
    p.wait("q", "dq", p.val("dq"))
    return p
